# version 24
# baseline (speedup 1.0000x reference)
import numpy as np
import ml_dtypes
import concourse.bass as bass
import concourse.mybir as mybir
from concourse.bass_utils import run_bass_kernel_spmd

F32 = mybir.dt.float32
BF16 = mybir.dt.bfloat16
AF = mybir.ActivationFunctionType
ALU = mybir.AluOpType
AX = mybir.AxisListType
bfd = ml_dtypes.bfloat16

B, T, C = 8, 2048, 2048
H, KVH, DH = 8, 2, 256
NT = T // 128
NCH = C // 128
NEG = -1.0e9
SCALE = 1.0 / 16.0

VENGS = ("sync", "tensor", "vector", "scalar", "gp", "gpd", "scd")
PHYS = {"sync": "sync", "tensor": "tensor", "vector": "vector",
        "scalar": "scalar", "gp": "gpsimd", "gpd": "gpsimd",
        "scd": "scalar"}


class _Op:
    __slots__ = ("veng", "fn", "dma", "stream", "deps", "dwaits",
                 "need_inc", "val", "idx", "label")

    def __init__(self, veng, fn, dma, stream):
        self.veng = veng
        self.fn = fn
        self.dma = dma
        self.stream = stream
        self.deps = set()
        self.dwaits = {}
        self.need_inc = False
        self.val = None
        self.idx = None
        self.label = None


class Rec:

    def __init__(self):
        self.phys_ops = {"sync": [], "tensor": [], "vector": [],
                         "scalar": [], "gpsimd": []}
        self.acc = {}
        self.fence_pending = {}
        self.stream_cnt = {}
        self.n_ops = 0

    def op(self, veng, fn, reads=(), writes=(), dma=False, stream=None,
           force_inc=False, label=None):
        assert dma == (stream is not None)
        o = _Op(veng, fn, dma, stream)
        o.label = label
        ph = PHYS[veng]
        deps = set()
        for t in reads:
            st = self.acc.get(t)
            if st:
                deps.update(st["w"].values())
        for t in writes:
            st = self.acc.get(t)
            if st:
                deps.update(st["w"].values())
                deps.update(st["r"])
        if ph in self.fence_pending:
            fence_ops, fence_streams = self.fence_pending.pop(ph)
            deps.update(fence_ops)
            for s, c in fence_streams.items():
                o.dwaits[s] = max(o.dwaits.get(s, 0), c)
        for p in deps:
            if p.dma:
                if o.stream is not None and p.stream == o.stream:
                    continue
                c = self.stream_cnt[p.stream]
                o.dwaits[p.stream] = max(o.dwaits.get(p.stream, 0), c)
            elif ph == "tensor" and PHYS[p.veng] == "tensor":
                pass
            else:
                o.deps.add(p)
                p.need_inc = True
        if dma:
            self.stream_cnt[stream] = self.stream_cnt.get(stream, 0) + 1
        if force_inc:
            o.need_inc = True
        for t in writes:
            st = self.acc.setdefault(t, {"w": {}, "r": []})
            st["w"] = {ph: o}
            st["r"] = []
        for t in reads:
            st = self.acc.setdefault(t, {"w": {}, "r": []})
            st["r"].append(o)
        self.phys_ops[ph].append(o)
        self.n_ops += 1
        return o

    def barrier(self):
        lasts = []
        for ph, ops in self.phys_ops.items():
            for o in reversed(ops):
                if not o.dma:
                    o.need_inc = True
                    lasts.append(o)
                    break
        snap = dict(self.stream_cnt)
        for ph in self.phys_ops:
            self.fence_pending[ph] = (set(lasts), snap)
        self.acc = {}

    def emit(self, nc, block, sems, dma_sems):
        cnt = {v: 0 for v in VENGS}
        for ph, ops in self.phys_ops.items():
            for o in ops:
                if o.need_inc:
                    cnt[o.veng] += 1
                    o.val = cnt[o.veng]

        def run_stream(eng, ph):
            guar = {}
            def wait(sem, val):
                k = id(sem)
                if guar.get(k, -1) < val:
                    eng.wait_ge(sem, val)
                    guar[k] = val
            for o in self.phys_ops[ph]:
                waits = {}
                for p in o.deps:
                    assert p.val is not None, "dep without inc"
                    waits[p.veng] = max(waits.get(p.veng, 0), p.val)
                for v, val in sorted(waits.items()):
                    wait(sems[v], val)
                for s, c in sorted(o.dwaits.items()):
                    wait(dma_sems[s], 16 * c)
                inst = o.fn(eng)
                if o.dma:
                    inst.then_inc(dma_sems[o.stream], 16)
                elif o.need_inc:
                    inst.then_inc(sems[o.veng], 1)

        @block.sync
        def _(sync):
            run_stream(sync, "sync")

        @block.tensor
        def _(tensor):
            run_stream(tensor, "tensor")

        @block.vector
        def _(vector):
            run_stream(vector, "vector")

        @block.scalar
        def _(scalar):
            run_stream(scalar, "scalar")

        @block.gpsimd
        def _(gpsimd):
            run_stream(gpsimd, "gpsimd")


def host_constants():
    d = {}
    inv = 1.0 / (10000.0 ** (np.arange(0, DH, 2, dtype=np.float64) / DH))
    t = np.arange(T, dtype=np.float64)
    f = t[:, None] * inv[None, :]
    d["cosT"] = np.cos(f).T.astype(bfd).copy()
    d["sinT"] = np.sin(f).T.astype(bfd).copy()
    d["ident"] = np.eye(128, dtype=bfd)
    trig = np.array([[sx, sy, sz] for sx in (1.0, -1.0) for sy in (1.0, -1.0)
                     for sz in (1.0, -1.0)], dtype=np.float64) / np.sqrt(3.0)
    dirs = np.zeros((128, 8), dtype=bfd)
    dirs[0:3, :] = trig.T.astype(bfd)
    d["dirs"] = dirs
    d["ones1"] = np.ones((1, 128), dtype=bfd)
    s = np.arange(512)[None, :]
    p = np.arange(128)[:, None]
    m = np.concatenate([np.where(s <= 128 * j + p, 0.0, NEG) for j in range(4)],
                       axis=1)
    d["maskS"] = m.astype(np.float32)
    d["maskG"] = m.astype(bfd)
    nb = np.empty((128, 2), dtype=np.float32)
    nb[:, 0] = -16.0
    nb[:, 1] = -20.0
    d["negb"] = nb
    return d


_CONSTS = host_constants()
_CACHED = {}


P3_SERIAL = False


def build_program(debug_outs=()):
    nc = bass.Bass("TRN2", target_bir_lowering=False, debug=False)
    ap = {}
    def inp(name, shape, dt):
        ap[name] = nc.dram_tensor(name, shape, dt, kind="ExternalInput").ap()
    inp("X", [T, C], F32)
    inp("WQ", [C, C], F32)
    inp("WK", [C, KVH * DH], F32)
    inp("WV", [C, KVH * DH], F32)
    inp("WO", [C, C], F32)
    inp("WF", [C, C], F32)
    inp("BIAS", [1, C], F32)
    inp("HS", [128, H], F32)
    inp("cosT", [128, T], BF16)
    inp("sinT", [128, T], BF16)
    inp("ident", [128, 128], BF16)
    inp("dirs", [128, H], BF16)
    inp("ones1", [1, 128], BF16)
    inp("maskS", [128, 2048], F32)
    inp("maskG", [128, 2048], BF16)
    inp("negb", [128, 2], F32)
    ap["OUT"] = nc.dram_tensor("OUT", [T, C], F32, kind="ExternalOutput").ap()
    def scratch(name, shape):
        kind = "ExternalOutput" if name in debug_outs else "Internal"
        ap[name] = nc.dram_tensor(name, shape, BF16, kind=kind).ap()
    scratch("xn_dram", [T, C])
    scratch("xnt2", [NT, 128, NCH, 128])
    scratch("qf_dram", [C, C])
    scratch("qfT_dram", [NCH, 128, NCH, 128])
    scratch("q_dram", [4, NCH, 128, 512])
    scratch("kpb_dram", [H, 128, T])
    scratch("v_dram", [KVH * DH, T])
    scratch("ao2", [NT, 128, NCH, 128])
    scratch("vT_dram", [T, KVH * DH])
    if "dbg_stats" in debug_outs:
        ap["dbg_stats"] = nc.dram_tensor("dbg_stats", [NT, 128, 16], F32,
                                         kind="ExternalOutput").ap()
    if "dbg_kt" in debug_outs:
        ap["dbg_kt"] = nc.dram_tensor("dbg_kt", [128, 4 * T], BF16,
                                      kind="ExternalOutput").ap()
        ap["dbg_qp"] = nc.dram_tensor("dbg_qp", [128, 128], BF16,
                                      kind="ExternalOutput").ap()
    if "dbg_vtok" in debug_outs:
        ap["dbg_vtok"] = nc.dram_tensor("dbg_vtok", [128, NT * 512], BF16,
                                        kind="ExternalOutput").ap()
        ap["dbg_vtok2"] = nc.dram_tensor("dbg_vtok2", [128, NT * 512], BF16,
                                         kind="ExternalOutput").ap()

    rec = Rec()

    from contextlib import ExitStack
    with ExitStack() as _ctx:
        ec = _ctx.enter_context
        sb_id = ec(nc.sbuf_tensor("sb_id", [128, 128], BF16))
        sb_ones = ec(nc.sbuf_tensor("sb_ones", [1, 128], BF16))
        sb_dirs = ec(nc.sbuf_tensor("sb_dirs", [128, H], BF16))
        sb_hs = ec(nc.sbuf_tensor("sb_hs", [128, H], F32))
        sb_hs2 = ec(nc.sbuf_tensor("sb_hs2", [128, 2 * H], F32))
        sb_negb = ec(nc.sbuf_tensor("sb_negb", [128, 2], F32))
        sb_sums = ec(nc.sbuf_tensor("sb_sums", [128, 16], F32))
        sb_qp = ec(nc.sbuf_tensor("sb_qp", [128, 128], F32))
        sb_bias = ec(nc.sbuf_tensor("sb_bias", [1, C], BF16))
        sb_maskS = ec(nc.sbuf_tensor("sb_maskS", [128, 2048], F32))
        sb_maskG = ec(nc.sbuf_tensor("sb_maskG", [128, 2048], BF16))
        sb_small = ec(nc.sbuf_tensor("sb_small", [128, 64], F32))
        sb_kprow = ec(nc.sbuf_tensor("sb_kprow", [1, T], BF16))
        sb_cos = ec(nc.sbuf_tensor("sb_cos", [128, T], BF16))
        sb_sin = ec(nc.sbuf_tensor("sb_sin", [128, T], BF16))
        kT = ec(nc.sbuf_tensor("kT", [128, 4 * T], BF16))
        vtok = ec(nc.sbuf_tensor("vtok", [128, NT * 512], BF16))
        AR16 = ec(nc.sbuf_tensor("AR16", [128, 54272], BF16))
        AR32 = ec(nc.sbuf_tensor("AR32", [128, 10240], F32))
        ps_s = ec(nc.psum_tensor("ps_s", [128, 2048], F32))
        ps_pt0 = ec(nc.psum_tensor("ps_pt0", [128, 512], F32))
        ps_pt1 = ec(nc.psum_tensor("ps_pt1", [128, 512], F32))
        ps_ov0 = ec(nc.psum_tensor("ps_ov0", [128, 512], F32))
        ps_ov1 = ec(nc.psum_tensor("ps_ov1", [128, 512], F32))
        s_sync = ec(nc.semaphore("s_sync"))
        s_tensor = ec(nc.semaphore("s_tensor"))
        s_vector = ec(nc.semaphore("s_vector"))
        s_scalar = ec(nc.semaphore("s_scalar"))
        s_gp = ec(nc.semaphore("s_gp"))
        s_gpd = ec(nc.semaphore("s_gpd"))
        s_scd = ec(nc.semaphore("s_scd"))
        block = ec(nc.Block())
        sems = {"sync": s_sync, "tensor": s_tensor, "vector": s_vector,
                "scalar": s_scalar, "gp": s_gp, "gpd": s_gpd, "scd": s_scd}

        def a16(lo, n):
            return AR16[:, lo:lo + n]

        def a32(lo, n):
            return AR32[:, lo:lo + n]

        small_slot = {}

        def slot(name):
            if name not in small_slot:
                small_slot[name] = len(small_slot)
            i = small_slot[name]
            assert i < 64
            return sb_small[:, i:i + 1]

        def dma(dst, src, reads, writes, q="sync", force_inc=False, stream=None):
            rec.op("gpd" if q == "gpd" else "sync",
                   lambda e, d=dst, s=src: e.dma_start(out=d, in_=s),
                   reads=reads, writes=writes, dma=True,
                   stream=stream or writes[0])

        def dmat(dst, src, reads, writes, stream=None, q="sync"):
            rec.op(q,
                   lambda e, d=dst, s=src: e.dma_start(out=d, in_=s, transpose=True),
                   reads=reads, writes=writes, dma=True,
                   stream=stream or writes[0])

        def mm(out, lhsT, rhs, start, stop, reads, writes, label=None):
            rec.op("tensor",
                   lambda e, o=out, l=lhsT, r=rhs, st=start, sp=stop:
                       e.matmul(o, l, r, start=st, stop=sp),
                   reads=reads, writes=writes, label=label)

        def vop(fn, reads, writes):
            rec.op("vector", fn, reads=reads, writes=writes)

        def sop(fn, reads, writes):
            rec.op("scalar", fn, reads=reads, writes=writes)

        dma(sb_id[:], ap["ident"][:], ["ident"], ["sb_id"], stream="init")
        dma(sb_dirs[:], ap["dirs"][:], ["dirs"], ["sb_dirs"], stream="init")
        dma(sb_ones[:], ap["ones1"][:], ["ones1"], ["sb_ones"], stream="init")
        dma(sb_hs[:], ap["HS"][:], ["HS"], ["sb_hs"], stream="init")
        dma(sb_maskS[:], ap["maskS"][:], ["maskS"], ["sb_maskS"], stream="init")
        dma(sb_maskG[:], ap["maskG"][:], ["maskG"], ["sb_maskG"], stream="init")
        dma(sb_cos[:], ap["cosT"][:], ["cosT"], ["sb_cos"], stream="init")
        dma(sb_sin[:], ap["sinT"][:], ["sinT"], ["sb_sin"], stream="init")
        dma(sb_bias[:], ap["BIAS"][:], ["BIAS"], ["sb_bias"], q="gpd")
        dma(sb_negb[:], ap["negb"][:], ["negb"], ["sb_negb"], stream="init")
        hs2v = sb_hs2[:].rearrange("p (a b) -> p a b", b=2)
        vop(lambda e: e.tensor_scalar(hs2v[:, :, 0], sb_hs[:], -1.0, 1.0,
                                      ALU.mult, ALU.add),
            ["sb_hs"], ["sb_hs2"])
        vop(lambda e: e.tensor_copy(hs2v[:, :, 1], sb_hs[:]),
            ["sb_hs"], ["sb_hs2"])

        WK, WV = 32768, 40960
        WQB = [16384, 24576]

        def cast_quarter(hp, emit=True):
            qb = hp % 2
            ths = []
            for cch in range(NCH):
                def th(cch=cch, qb=qb, hp=hp):
                    dma(a16(WQB[qb] + cch * 512, 512),
                        ap["WQ"][cch * 128:(cch + 1) * 128,
                                 hp * 512:(hp + 1) * 512],
                        ["WQ"], [f"wq{qb}"], q="gpd", stream=f"wq{qb}")
                if emit:
                    th()
                else:
                    ths.append(th)
            return ths

        pending_casts = list(cast_quarter(0, emit=False))
        for cch in range(NCH):
            def thk(cch=cch):
                dma(a16(WK + cch * 512, 512),
                    ap["WK"][cch * 128:(cch + 1) * 128, :],
                    ["WK"], ["wk"], q="gpd")
            pending_casts.append(thk)
        for cch in range(NCH):
            def thv(cch=cch):
                dma(a16(WV + cch * 512, 512),
                    ap["WV"][cch * 128:(cch + 1) * 128, :],
                    ["WV"], ["wv"], q="gpd")
            pending_casts.append(thv)
        pending_casts += cast_quarter(1, emit=False)

        X_TILE = [0, 2048, 4096, 6144]
        WF_TILE = [8192]
        XN_BF = [0, 2048, 4096, 6144]
        GTP = [8192, 10240]
        LTN = [12288, 14336]

        def p1_stats(grp4):
            gb = grp4 % 2
            st = f"st{gb}"
            ssq4 = sb_small[:, gb * 32:gb * 32 + 4]
            sm4 = sb_small[:, gb * 32 + 4:gb * 32 + 8]
            mean4 = sb_small[:, gb * 32 + 8:gb * 32 + 12]
            msq4 = sb_small[:, gb * 32 + 12:gb * 32 + 16]
            varn4 = sb_small[:, gb * 32 + 16:gb * 32 + 20]
            rstd4 = sb_small[:, gb * 32 + 20:gb * 32 + 24]
            negmr4 = sb_small[:, gb * 32 + 24:gb * 32 + 28]
            vop(lambda e, o=mean4, x=sm4: e.tensor_scalar(o, x, 1.0 / C, None,
                                                          ALU.mult), [st], [st])
            vop(lambda e, o=msq4, x=mean4: e.tensor_tensor(o, x, x, ALU.mult),
                [st], [st])
            vop(lambda e, o=varn4, x=ssq4: e.tensor_scalar(o, x, 1.0 / C, None,
                                                           ALU.mult), [st], [st])
            vop(lambda e, o=varn4, x=msq4: e.tensor_tensor(o, o, x, ALU.subtract),
                [st], [st])
            vop(lambda e, o=varn4: e.tensor_scalar(o, o, 1e-5, None, ALU.add),
                [st], [st])
            sop(lambda e, o=rstd4, x=varn4: e.activation(o, x, AF.Sqrt),
                [st], [st])
            vop(lambda e, o=rstd4: e.reciprocal(o, o), [st], [st])
            vop(lambda e, o=negmr4, m=mean4, r=rstd4:
                    e.tensor_tensor(o, m, r, ALU.mult), [st], [st])
            vop(lambda e, o=negmr4: e.tensor_scalar(o, o, -1.0, None, ALU.mult),
                [st], [st])
            return rstd4, negmr4

        for i in range(NT):
            grp4, ii = i // 4, i % 4
            gb = grp4 % 2
            st = f"st{gb}"
            xb = i % 4
            xt = a32(X_TILE[xb], 2048)
            xt_t = f"xt{xb}"
            dma(xt, ap["X"][i * 128:(i + 1) * 128, :], ["X"], [xt_t])
            xnb = a16(XN_BF[xb], 2048)
            xnb_t = f"xnb{xb}"
            ssqc = sb_small[:, gb * 32 + ii:gb * 32 + ii + 1]
            smc = sb_small[:, gb * 32 + 4 + ii:gb * 32 + 4 + ii + 1]
            sop(lambda e, o=xnb, x=xt, a=ssqc: e.activation(o, x, AF.Square,
                                                            accum_out=a),
                [xt_t], [xnb_t, st])
            sop(lambda e, o=xnb, x=xt, a=smc: e.activation(o, x, AF.Identity,
                                                           accum_out=a),
                [xt_t], [xnb_t, st])
            if ii == 3:
                rstd4, negmr4 = p1_stats(grp4)
                for jj in range(4):
                    i2 = grp4 * 4 + jj
                    xb2 = i2 % 4
                    xt2 = a32(X_TILE[xb2], 2048)
                    xnb2 = a16(XN_BF[xb2], 2048)
                    xnb2_t = f"xnb{xb2}"
                    sop(lambda e, o=xnb2, x=xt2, r=rstd4[:, jj:jj + 1],
                        b=negmr4[:, jj:jj + 1]:
                            e.activation(o, x, AF.Identity, bias=b, scale=r),
                        [f"xt{xb2}", st], [xnb2_t])
                    xsg = vtok[:, (jj % 2) * 2048:(jj % 2) * 2048 + 2048]
                    xsg_t = f"xstg{jj % 2}"
                    for grp in range(4):
                        psb = [ps_pt0, ps_pt1][grp % 2]
                        pst = f"p1pt{grp % 2}"
                        for cc in range(4):
                            cch = grp * 4 + cc
                            mm(psb[:, cc * 128:(cc + 1) * 128],
                               xnb2[:, cch * 128:(cch + 1) * 128], sb_id[:],
                               True, True, [xnb2_t, "sb_id"], [pst])
                        ev = xsg[:, grp * 512:(grp + 1) * 512]
                        if grp % 2 == 0:
                            sop(lambda e, o=ev, x=psb[:]: e.copy(o, x),
                                [pst], [xsg_t])
                        else:
                            vop(lambda e, o=ev, x=psb[:]: e.tensor_copy(o, x),
                                [pst], [xsg_t])
                    dma(ap["xnt2"][i2], xsg, [xsg_t], ["xnt2"], q="gpd", stream=xsg_t)
            wt = a32(WF_TILE[0], 2048)
            wt_t = "wt0"
            dma(wt, ap["WF"][i * 128:(i + 1) * 128, :], ["WF"], [wt_t])
            wb = i % 2
            ths = sb_small[:, 60:61]
            thr = sb_small[:, 61:62]
            nthr = sb_small[:, 62:63]
            vop(lambda e, o=ths, x=wt: e.reduce_sum(o, x, AX.X,
                                                    apply_absolute_value=True),
                [wt_t], ["ths"])
            vop(lambda e, o=thr, x=ths: e.tensor_scalar(o, x, 0.7 / C, None, ALU.mult),
                ["ths"], ["thr"])
            vop(lambda e, o=nthr, x=thr: e.tensor_scalar(o, x, -1.0, None, ALU.mult),
                ["thr"], ["nthr"])
            gtp = a16(GTP[wb], 2048)
            ltn = a16(LTN[wb], 2048)
            vop(lambda e, o=gtp, x=wt, t=thr: e.tensor_scalar(o, x, t, None, ALU.is_gt),
                [wt_t, "thr"], [f"gtp{wb}"])
            vop(lambda e, o=ltn, x=wt, t=nthr: e.tensor_scalar(o, x, t, None, ALU.is_lt),
                [wt_t, "nthr"], [f"ltn{wb}"])
            vop(lambda e, o=gtp, x=ltn: e.tensor_tensor(o, o, x, ALU.subtract),
                [f"gtp{wb}", f"ltn{wb}"], [f"gtp{wb}"])
            qsg = vtok[:, 4096 + wb * 2048:4096 + wb * 2048 + 2048]
            qsg_t = f"qstg{wb}"
            for grp in range(4):
                psb = [ps_pt0, ps_pt1][grp % 2]
                pst = f"p1pt{grp % 2}"
                for cc in range(4):
                    ich = grp * 4 + cc
                    mm(psb[:, cc * 128:(cc + 1) * 128],
                       gtp[:, ich * 128:(ich + 1) * 128], sb_id[:],
                       True, True, [f"gtp{wb}", "sb_id"], [pst])
                ev = qsg[:, grp * 512:(grp + 1) * 512]
                if grp % 2 == 0:
                    sop(lambda e, o=ev, x=psb[:]: e.copy(o, x), [pst], [qsg_t])
                else:
                    vop(lambda e, o=ev, x=psb[:]: e.tensor_copy(o, x), [pst], [qsg_t])
            dma(ap["qfT_dram"][i], qsg, [qsg_t], ["qfT_dram"], q="gpd", stream=qsg_t)
            for th in pending_casts[i * 4:i * 4 + 4]:
                th()
        rec.barrier()

        XNT = [0, 8192]
        WRK = 49152
        RW = {}
        for par in range(2):
            base = WRK + par * 2560
            RW[par] = dict(
                qa_raw=a16(base + 0, 512), qb_raw=a16(base + 512, 512),
                qa2=a16(base + 1024, 512), qb2=a16(base + 1536, 512),
                tr1=a16(base + 2048, 512))

        qp_pending = []
        for hp in range(4):
            qb = hp % 2
            for tt5 in range(4):
                if tt5 == 1 and 1 <= hp < 3:
                    cast_quarter(hp + 1)
                xb = (hp * 4 + tt5) % 2
                xnt = a16(XNT[xb], 8192)
                xnt_t = f"xnt{xb}"
                xnt4 = xnt.rearrange("p (c gg t) -> p c gg t", c=16, gg=4)
                for gg in range(4):
                    dma(xnt4[:, :, gg, :], ap["xnt2"][tt5 * 4 + gg],
                        ["xnt2"], [xnt_t], q="sync" if gg % 2 == 0 else "scd")
                cs = sb_cos[:, tt5 * 512:(tt5 + 1) * 512]
                sn = sb_sin[:, tt5 * 512:(tt5 + 1) * 512]

                def rope_pair(par, psA, psB, tA, tB, outA, outA_t, outB, outB_t,
                              cs=cs, sn=sn):
                    w = RW[par]
                    qa_raw, qb_raw = w["qa_raw"], w["qb_raw"]
                    qa2, qb2, tr1 = w["qa2"], w["qb2"], w["tr1"]
                    ta, tb = f"qa_raw{par}", f"qb_raw{par}"
                    t2a, t2b, ttr = f"qa2_{par}", f"qb2_{par}", f"tr1_{par}"
                    sop(lambda e, o=qa_raw, x=psA: e.copy(o, x), [tA], [ta])
                    sop(lambda e, o=qb_raw, x=psB: e.copy(o, x), [tB], [tb])
                    vop(lambda e, o=qa2, x=qa_raw, c=cs: e.tensor_tensor(o, x, c, ALU.mult),
                        [ta, "sb_cos"], [t2a])
                    vop(lambda e, o=tr1, x=qb_raw, c=sn: e.tensor_tensor(o, x, c, ALU.mult),
                        [tb, "sb_sin"], [ttr])
                    vop(lambda e, o=outA, x=qa2, y=tr1: e.tensor_tensor(o, x, y, ALU.subtract),
                        [t2a, ttr], [outA_t])
                    vop(lambda e, o=qb2, x=qb_raw, c=cs: e.tensor_tensor(o, x, c, ALU.mult),
                        [tb, "sb_cos"], [t2b])
                    vop(lambda e, o=tr1, x=qa_raw, c=sn: e.tensor_tensor(o, x, c, ALU.mult),
                        [ta, "sb_sin"], [ttr])
                    vop(lambda e, o=outB, x=qb2, y=tr1: e.tensor_tensor(o, x, y, ALU.add),
                        [t2b, ttr], [outB_t])

                for hh in range(2):
                    h = 2 * hp + hh
                    par = hh
                    psA = ps_s[:, par * 1024:par * 1024 + 512]
                    psB = ps_s[:, par * 1024 + 512:par * 1024 + 1024]
                    tA, tB = f"psq{par}a", f"psq{par}b"
                    for cch in range(NCH):
                        lA = a16(WQB[qb] + cch * 512 + hh * 256, 128)
                        lB = a16(WQB[qb] + cch * 512 + hh * 256 + 128, 128)
                        r = a16(XNT[xb] + cch * 512, 512)
                        mm(psA, lA, r, cch == 0, cch == NCH - 1,
                           [f"wq{qb}", xnt_t], [tA])
                        mm(psB, lB, r, cch == 0, cch == NCH - 1,
                           [f"wq{qb}", xnt_t], [tB])
                    for th in qp_pending:
                        th()
                    qp_pending = []
                    qa2, qb2 = RW[par]["qa2"], RW[par]["qb2"]
                    t2a, t2b = f"qa2_{par}", f"qb2_{par}"
                    rope_pair(par, psA, psB, tA, tB, qa2, t2a, qb2, t2b)
                    dma(ap["q_dram"][tt5, 2 * h], qa2, [t2a], ["q_dram"], q="gpd", stream=t2a)
                    dma(ap["q_dram"][tt5, 2 * h + 1], qb2, [t2b], ["q_dram"], q="gpd", stream=t2b)

                    def qp_thunk(h=h, hh=hh, tt5=tt5, qa2=qa2, t2a=t2a):
                        for j in range(4):
                            mm(ps_pt0[:, hh * 4 + j:hh * 4 + j + 1],
                               qa2[:, j * 128:(j + 1) * 128],
                               sb_dirs[:, h:h + 1], True, True,
                               [t2a, "sb_dirs"], ["psqp"])
                        vop(lambda e,
                            o=sb_qp[:, h * 16 + tt5 * 4:h * 16 + tt5 * 4 + 4],
                            x=ps_pt0[:, hh * 4:hh * 4 + 4]: e.tensor_copy(o, x),
                            ["psqp"], ["sb_qp"])
                    qp_pending = [qp_thunk]
                if hp == 0:
                    for kvp in range(KVH):
                        par = kvp % 2
                        psA = ps_s[:, par * 1024:par * 1024 + 512]
                        psB = ps_s[:, par * 1024 + 512:par * 1024 + 1024]
                        tA, tB = f"psq{par}a", f"psq{par}b"
                        for cch in range(NCH):
                            lA = a16(WK + cch * 512 + kvp * 256, 128)
                            lB = a16(WK + cch * 512 + kvp * 256 + 128, 128)
                            r = a16(XNT[xb] + cch * 512, 512)
                            mm(psA, lA, r, cch == 0, cch == NCH - 1,
                               ["wk", xnt_t], [tA])
                            mm(psB, lB, r, cch == 0, cch == NCH - 1,
                               ["wk", xnt_t], [tB])
                        oA = kT[:, (2 * kvp) * T + tt5 * 512:(2 * kvp) * T + (tt5 + 1) * 512]
                        oB = kT[:, (2 * kvp + 1) * T + tt5 * 512:(2 * kvp + 1) * T + (tt5 + 1) * 512]
                        rope_pair(par, psA, psB, tA, tB, oA, "kT", oB, "kT")
                    for vf in range(4):
                        for cch in range(NCH):
                            l = a16(WV + cch * 512 + vf * 128, 128)
                            r = a16(XNT[xb] + cch * 512, 512)
                            mm(ps_ov1[:], l, r, cch == 0, cch == NCH - 1,
                               ["wv", xnt_t], ["psv"])
                        vs = vtok[:, vf * 512:(vf + 1) * 512]
                        vs_t = f"vs4_{vf}"
                        sop(lambda e, o=vs, x=ps_ov1[:]: e.copy(o, x),
                            ["psv"], [vs_t])
                    for tb4 in range(4):
                        for vf in range(4):
                            mm(ps_pt1[:, vf * 128:(vf + 1) * 128],
                               vtok[:, vf * 512 + tb4 * 128:
                                     vf * 512 + (tb4 + 1) * 128],
                               sb_id[:], True, True,
                               [f"vs4_{vf}", "sb_id"], ["p2vt"])
                        vts = vtok[:, 2048 + (tb4 % 2) * 512:
                                   2048 + (tb4 % 2) * 512 + 512]
                        vts_t = f"vts{tb4 % 2}"
                        if tb4 % 2 == 0:
                            sop(lambda e, o=vts: e.copy(o, ps_pt1[:]),
                                ["p2vt"], [vts_t])
                        else:
                            vop(lambda e, o=vts: e.tensor_copy(o, ps_pt1[:]),
                                ["p2vt"], [vts_t])
                        dma(ap["vT_dram"][tt5 * 512 + tb4 * 128:
                                          tt5 * 512 + (tb4 + 1) * 128, :],
                            vts, [vts_t], ["vT_dram"], q="gpd", stream=vts_t)
        for th in qp_pending:
            th()
        qp_pending = []
        rec.barrier()
        kpb_sts = [a16(0, 2048), a16(2048, 2048)]
        for h in range(H):
            kpb_st = kpb_sts[h % 2]
            kpb_st_t = f"kpb_st{h % 2}"
            kvc = 2 * (h // 4)
            for tt5 in range(4):
                mm(ps_pt1[0:1, 0:512], sb_dirs[:, h:h + 1],
                   kT[:, kvc * T + tt5 * 512:kvc * T + (tt5 + 1) * 512],
                   True, True, ["sb_dirs", "kT"], ["pskp"])
                vop(lambda e, o=sb_kprow[0:1, tt5 * 512:(tt5 + 1) * 512],
                    x=ps_pt1[0:1, 0:512]: e.tensor_copy(o, x),
                    ["pskp"], ["kprow"])
            for tt5 in range(4):
                mm(ps_ov0[:], sb_ones[:], sb_kprow[0:1, tt5 * 512:(tt5 + 1) * 512],
                   True, True, ["sb_ones", "kprow"], ["pskpb"])
                sop(lambda e, o=kpb_st[:, tt5 * 512:(tt5 + 1) * 512], x=ps_ov0[:]:
                        e.copy(o, x), ["pskpb"], [kpb_st_t])
            dma(ap["kpb_dram"][h], kpb_st, [kpb_st_t], ["kpb_dram"], q="gpd", stream=kpb_st_t)
        rec.barrier()

        if "dbg_kt" in ap:
            dma(ap["dbg_kt"][:], kT[:], ["kT"], ["dbg_kt"], q="gpd")
            dma(ap["dbg_qp"][:], sb_qp[:], ["sb_qp"], ["dbg_qp"], q="gpd")
            rec.barrier()
        PC = 0
        PTG = 16384
        QG = 24576
        KPB = [32768, 34816]
        EXPS = [36864, 38912]
        RAWG = [40960, 43008]
        EXPG = [45056, 47104]
        AOST = [49152, 50176]

        for tc in range(NT):
            dma(vtok[:, tc * 512:(tc + 1) * 512],
                ap["vT_dram"][tc * 128:(tc + 1) * 128, :],
                ["vT_dram"], ["vtok"])

        if "dbg_vtok" in ap:
            rec.barrier()
            dma(ap["dbg_vtok"][:], vtok[:], ["vtok"], ["dbg_vtok"], q="gpd")
            rec.barrier()

        def softmax_A(g, h, j):
            qt = 4 * g + j
            sset = h % 2
            jb = j % 2
            ndw = g * 512
            klen = (g + 1) * 512
            qg_lA = a16(QG + (2 * h) * 512 + j * 128, 128)
            qg_lB = a16(QG + (2 * h + 1) * 512 + j * 128, 128)
            kvc = 2 * (h // 4)
            exps = a16(EXPS[jb], 2048)
            exps_t = f"expS{jb}"
            sums = sb_sums[:, 8 * jb:8 * jb + 8]
            sums_t = f"sums{jb}"
            sbase = jb * 2 if g <= 1 else 0
            for kt in range(g + 1):
                rhsA = kT[:, kvc * T + kt * 512:kvc * T + (kt + 1) * 512]
                rhsB = kT[:, (kvc + 1) * T + kt * 512:(kvc + 1) * T + (kt + 1) * 512]
                o = ps_s[:, (sbase + kt) * 512:(sbase + kt + 1) * 512]
                bt = "ps_s%d" % (sbase + kt)
                mm(o, qg_lA, rhsA, True, False, ["q_g", "kT"], [bt])
                mm(o, qg_lB, rhsB, False, True, ["q_g", "kT"], [bt])
                if kt == g:
                    vop(lambda e, o=o: e.tensor_tensor(
                            o, o, sb_maskS[:, j * 512:(j + 1) * 512], ALU.add),
                        [bt, "sb_maskS"], [bt])
                sop(lambda e, o=exps[:, kt * 512:(kt + 1) * 512], x=o,
                    a=sums[:, kt:kt + 1]:
                        e.activation(o, x, AF.Exp, bias=sb_negb[:, 0:1],
                                     scale=SCALE, accum_out=a),
                    [bt, "sb_negb"], [exps_t, sums_t])
            rg = a16(RAWG[jb], 2048)
            rg_t = f"rawG{jb}"
            qp_col = sb_qp[:, h * 16 + qt:h * 16 + qt + 1]
            kpb = a16(KPB[sset], 2048)
            vop(lambda e, o=rg[:, 0:klen], x=kpb[:, 0:klen], s=qp_col:
                    e.tensor_scalar(o, x, s, None, ALU.mult),
                [f"kpb{sset}", "sb_qp"], [rg_t])
            vop(lambda e, o=rg[:, ndw:klen]:
                    e.tensor_tensor(o, o, sb_maskG[:, j * 512:(j + 1) * 512],
                                    ALU.add),
                [rg_t, "sb_maskG"], [rg_t])
            expg = a16(EXPG[jb], 2048)
            expg_t = f"expG{jb}"
            sop(lambda e, o=expg[:, 0:klen], x=rg[:, 0:klen], a=sums[:, 5:6]:
                    e.activation(o, x, AF.Exp, bias=sb_negb[:, 1:2], accum_out=a),
                [rg_t, "sb_negb"], [expg_t, sums_t])
        def softmax_B(g, h, j):
            qt = 4 * g + j
            sset = h % 2
            jb = j % 2
            klen = (g + 1) * 512
            exps = a16(EXPS[jb], 2048)
            exps_t = f"expS{jb}"
            expg = a16(EXPG[jb], 2048)
            expg_t = f"expG{jb}"
            sums = sb_sums[:, 8 * jb:8 * jb + 8]
            sums_t = f"sums{jb}"
            if g == 0:
                vop(lambda e, o=sums[:, 4:5], x=sums[:, 0:1]: e.tensor_copy(o, x),
                    [sums_t], [sums_t])
            else:
                vop(lambda e, o=sums[:, 4:5], x=sums[:, 0:g + 1], gg=g:
                        e.reduce_sum(o, x, AX.X),
                    [sums_t], [sums_t])
            vop(lambda e, o=sums[:, 6:8], x=sums[:, 4:6]: e.reciprocal(o, x),
                [sums_t], [sums_t])
            vop(lambda e, o=sums[:, 6:8], x=sb_hs2[:, 2 * h:2 * h + 2]:
                    e.tensor_tensor(o, o, x, ALU.mult),
                [sums_t, "sb_hs2"], [sums_t])
            vop(lambda e, o=expg[:, 0:klen], s=sums[:, 7:8]:
                    e.tensor_scalar(o, o, s, None, ALU.mult),
                [expg_t, sums_t], [expg_t])
            pc = a16(PC + (sset * 4 + j) * 2048, 2048)
            vop(lambda e, o=exps[:, 0:klen], s=sums[:, 6:7]:
                    e.tensor_scalar(o, o, s, None, ALU.mult),
                [exps_t, sums_t], [exps_t])
            vop(lambda e, o=pc[:, 0:klen], x=exps[:, 0:klen], y=expg[:, 0:klen]:
                    e.tensor_tensor(o, x, y, ALU.add),
                [exps_t, expg_t], [f"pc{sset}_{j}"])

        def ptov_units(g, h):
            sset = h % 2
            kvc4 = (h // 4) * 256
            nkc = 4 * g + 4
            units = []

            def memsets():
                for j0 in range(1, 4):
                    kc = 4 * g + j0
                    o = a16(PTG + kc * 512, j0 * 128)
                    vop(lambda e, o=o: e.memset(o, 0.0), [], ["ptg"])
            units.append(memsets)

            def tp_unit(kc):
                j0 = max(0, kc - 4 * g)
                pst = [ps_pt0, ps_pt1][kc % 2]
                pst_t = f"ps_pt{kc % 2}"
                for j in range(j0, 4):
                    pc = a16(PC + (sset * 4 + j) * 2048, 2048)
                    mm(pst[:, j * 128:(j + 1) * 128],
                       pc[:, kc * 128:(kc + 1) * 128], sb_id[:],
                       True, True, [f"pc{sset}_{j}", "sb_id"], [pst_t])
                ev = a16(PTG + kc * 512 + j0 * 128, 512 - j0 * 128)
                if kc % 2 == 0:
                    sop(lambda e, o=ev, x=pst[:, j0 * 128:512]: e.copy(o, x),
                        [pst_t], ["ptg"])
                else:
                    vop(lambda e, o=ev, x=pst[:, j0 * 128:512]: e.tensor_copy(o, x),
                        [pst_t], ["ptg"])

            def ov_unit(kc):
                lhs0 = vtok[:, kc * 512 + kvc4:kc * 512 + kvc4 + 128]
                lhs1 = vtok[:, kc * 512 + kvc4 + 128:kc * 512 + kvc4 + 256]
                rhs = a16(PTG + kc * 512, 512)
                mm(ps_ov0[:], lhs0, rhs, kc == 0, kc == nkc - 1,
                   ["vtok", "ptg"], ["ps_ov0"])
                mm(ps_ov1[:], lhs1, rhs, kc == 0, kc == nkc - 1,
                   ["vtok", "ptg"], ["ps_ov1"])

            LAG = 4
            seq = []
            for kc in range(nkc):
                seq.append(("tp", kc))
                if kc >= LAG:
                    seq.append(("ov", kc - LAG))
            for kc in range(max(0, nkc - LAG), nkc):
                seq.append(("ov", kc))
            for kind, kc in seq:
                def unit(kind=kind, kc=kc):
                    (tp_unit if kind == "tp" else ov_unit)(kc)
                units.append(unit)

            def tail():
                ab = h % 2
                ao = a16(AOST[ab], 1024)
                ao_t = f"ao{ab}"
                sop(lambda e, o=ao[:, 0:512], x=ps_ov0[:]: e.copy(o, x),
                    ["ps_ov0"], [ao_t])
                sop(lambda e, o=ao[:, 512:1024], x=ps_ov1[:]: e.copy(o, x),
                    ["ps_ov1"], [ao_t])
                for dch in range(2):
                    src3 = ao[:, dch * 512:(dch + 1) * 512].rearrange(
                        "p (g2 t) -> p g2 t", g2=4)
                    dma(ap["ao2"][4 * g:4 * g + 4, :, 2 * h + dch]
                        .rearrange("g2 p t -> p g2 t"),
                        src3, [ao_t], ["ao2"], q="gpd", stream=ao_t)
            units.append(tail)
            return units

        pending = []
        for g in range(4):
            dma(a16(QG, 8192), ap["q_dram"][g].rearrange("f p c -> p f c"),
                ["q_dram"], ["q_g"])
            for h in range(H):
                sset = h % 2
                dma(a16(KPB[sset], 2048), ap["kpb_dram"][h],
                    ["kpb_dram"], [f"kpb{sset}"])
                np_slots = 9
                nchunk = (len(pending) + np_slots - 1) // np_slots if pending else 0
                pi = [0]

                def emit_pending():
                    for th in pending[pi[0]:pi[0] + nchunk]:
                        th()
                    pi[0] += nchunk
                for j in range(4):
                    emit_pending()
                    softmax_A(g, h, j)
                    if j > 0:
                        emit_pending()
                        softmax_B(g, h, j - 1)
                emit_pending()
                softmax_B(g, h, 2 + 1)
                for th in pending[pi[0]:]:
                    th()
                if P3_SERIAL:
                    rec.barrier()
                pending = ptov_units(g, h)
                if P3_SERIAL:
                    for th in pending:
                        th()
                    pending = []
                    rec.barrier()
        for th in pending:
            th()
        rec.barrier()
        if "dbg_vtok" in ap:
            dma(ap["dbg_vtok2"][:], vtok[:], ["vtok"], ["dbg_vtok2"], q="gpd")
            rec.barrier()

        NB4 = 5
        AO4 = [0, 2048, 4096, 6144, 8192]
        XNT4 = [10240, 12288, 14336, 16384, 18432]
        WOCT = [20480, 28672]
        QFT = [36864, 45056]
        XSL = [0, 2048, 4096, 6144, 8192]

        def p4_wloads(cb, ct):
            for ch in range(NCH):
                dma(a16(WOCT[cb] + ch * 512, 512),
                    ap["WO"][ch * 128:(ch + 1) * 128, ct * 512:(ct + 1) * 512],
                    ["WO"], [f"wo_ct{cb}"], q="gpd")
            qft4 = a16(QFT[cb], 8192).rearrange("p (c ot o) -> p c ot o",
                                                 c=16, ot=4, o=128)
            for ot in range(4):
                dma(qft4[:, :, ot, :], ap["qfT_dram"][ct * 4 + ot],
                    ["qfT_dram"], [f"qft_ct{cb}"], q="scd")
        p4_wloads(0, 0)
        p4_wloads(1, 1)
        for pp in range(2):
            order = list(range(NT)) if pp == 0 else                 list(range(11, 16)) + list(range(11))
            for idx, tt in enumerate(order):
                tb = tt % NB4
                sb_ao = a16(AO4[tb], 2048)
                sb_x4 = a16(XNT4[tb], 2048)
                ao_t, x4_t = f"ao4{tb}", f"x4{tb}"
                if not (pp == 1 and idx < 5):
                    dma(sb_ao, ap["ao2"][tt], ["ao2"], [ao_t])
                    dma(sb_x4, ap["xnt2"][tt], ["xnt2"], [x4_t])
                xs = a32(XSL[tb], 1024)
                xs_t = f"xs{tb}"
                dma(xs, ap["X"][tt * 128:(tt + 1) * 128,
                                pp * 1024:(pp + 1) * 1024],
                    ["X"], [xs_t])
                ob = a32(XSL[tb] + 1024, 1024)
                ob_t = f"ob{tb}"
                for ctl in range(2):
                    ct = 2 * pp + ctl
                    cb = ctl
                    pq = (2 * idx + ctl) % 4
                    ps = ps_s[:, pq * 512:(pq + 1) * 512]
                    ps_t = f"pso{pq}"
                    for ch in range(NCH):
                        mm(ps, sb_ao[:, ch * 128:(ch + 1) * 128],
                           a16(WOCT[cb] + ch * 512, 512), ch == 0, False,
                           [ao_t, f"wo_ct{cb}"], [ps_t],
                           label=f"p4mm_ct{ct}_tt{tt}_ch{ch}")
                    for ich in range(NCH):
                        mm(ps, sb_x4[:, ich * 128:(ich + 1) * 128],
                           a16(QFT[cb] + ich * 512, 512), False, False,
                           [x4_t, f"qft_ct{cb}"], [ps_t])
                    mm(ps, sb_ones[:], sb_bias[0:1, ct * 512:(ct + 1) * 512],
                       False, True, ["sb_ones", "sb_bias"], [ps_t])
                    vop(lambda e, o=ob[:, ctl * 512:(ctl + 1) * 512], p=ps,
                        x=xs[:, ctl * 512:(ctl + 1) * 512]:
                            e.tensor_tensor(o, p, x, ALU.add),
                        [ps_t, xs_t], [ob_t])
                dma(ap["OUT"][tt * 128:(tt + 1) * 128,
                              pp * 1024:(pp + 1) * 1024],
                    ob, [ob_t], ["OUT"], q="gpd", stream=ob_t)
            if pp == 0:
                p4_wloads(0, 2)
                p4_wloads(1, 3)
        rec.barrier()
        rec.op("vector", lambda e: e.nop())

        dma_sems = {s: ec(nc.semaphore(f"d_{s}"))
                    for s in sorted(rec.stream_cnt)}
        rec.emit(nc, block, sems, dma_sems)
    return nc


def _prepare(L0_attn_x, L0_attn_Wq, L0_attn_Wk, L0_attn_Wv, L0_attn_Wo,
             L0_attn_head_scales, L0_ffn_weight, L0_ffn_bias):
    x = np.ascontiguousarray(np.asarray(L0_attn_x, dtype=np.float32))
    wq = np.ascontiguousarray(np.asarray(L0_attn_Wq, dtype=np.float32))
    wk = np.ascontiguousarray(np.asarray(L0_attn_Wk, dtype=np.float32))
    wv = np.ascontiguousarray(np.asarray(L0_attn_Wv, dtype=np.float32))
    wo = np.ascontiguousarray(np.asarray(L0_attn_Wo, dtype=np.float32))
    hs = np.asarray(L0_attn_head_scales, dtype=np.float32)
    wf = np.ascontiguousarray(np.asarray(L0_ffn_weight, dtype=np.float32))
    bias = np.ascontiguousarray(np.asarray(L0_ffn_bias, dtype=np.float32)
                                .reshape(1, C))
    hs_rep = np.ascontiguousarray(np.broadcast_to(hs[None, :], (128, H)))
    base = {
        "WQ": wq, "WK": wk, "WV": wv, "WO": wo, "WF": wf, "BIAS": bias,
        "HS": hs_rep,
        "cosT": _CONSTS["cosT"], "sinT": _CONSTS["sinT"],
        "ident": _CONSTS["ident"], "dirs": _CONSTS["dirs"],
        "ones1": _CONSTS["ones1"], "maskS": _CONSTS["maskS"],
        "maskG": _CONSTS["maskG"], "negb": _CONSTS["negb"],
    }
    in_maps = []
    for c in range(B):
        m = dict(base)
        m["X"] = np.ascontiguousarray(x[c])
        in_maps.append(m)
    return in_maps


def kernel(**inputs):
    in_maps = _prepare(**inputs)
    if "nc" not in _CACHED:
        _CACHED["nc"] = build_program()
    nc = _CACHED["nc"]
    res = run_bass_kernel_spmd(nc, in_maps, core_ids=list(range(B)))
    out = np.stack([np.asarray(r["OUT"], dtype=np.float32)
                    for r in res.results], axis=0)
    return out


# version 25
# speedup vs baseline: 1.0039x; 1.0039x over previous
import numpy as np
import ml_dtypes
import concourse.bass as bass
import concourse.mybir as mybir
from concourse.bass_utils import run_bass_kernel_spmd

F32 = mybir.dt.float32
BF16 = mybir.dt.bfloat16
AF = mybir.ActivationFunctionType
ALU = mybir.AluOpType
AX = mybir.AxisListType
bfd = ml_dtypes.bfloat16

B, T, C = 8, 2048, 2048
H, KVH, DH = 8, 2, 256
NT = T // 128
NCH = C // 128
NEG = -1.0e9
SCALE = 1.0 / 16.0

VENGS = ("sync", "tensor", "vector", "scalar", "gp", "gpd", "scd")
PHYS = {"sync": "sync", "tensor": "tensor", "vector": "vector",
        "scalar": "scalar", "gp": "gpsimd", "gpd": "gpsimd",
        "scd": "scalar"}


class _Op:
    __slots__ = ("veng", "fn", "dma", "stream", "deps", "dwaits",
                 "need_inc", "val", "idx", "label")

    def __init__(self, veng, fn, dma, stream):
        self.veng = veng
        self.fn = fn
        self.dma = dma
        self.stream = stream
        self.deps = set()
        self.dwaits = {}
        self.need_inc = False
        self.val = None
        self.idx = None
        self.label = None


class Rec:

    def __init__(self):
        self.phys_ops = {"sync": [], "tensor": [], "vector": [],
                         "scalar": [], "gpsimd": []}
        self.acc = {}
        self.fence_pending = {}
        self.stream_cnt = {}
        self.n_ops = 0

    def op(self, veng, fn, reads=(), writes=(), dma=False, stream=None,
           force_inc=False, label=None):
        assert dma == (stream is not None)
        o = _Op(veng, fn, dma, stream)
        o.label = label
        ph = PHYS[veng]
        deps = set()
        for t in reads:
            st = self.acc.get(t)
            if st:
                deps.update(st["w"].values())
        for t in writes:
            st = self.acc.get(t)
            if st:
                deps.update(st["w"].values())
                deps.update(st["r"])
        if ph in self.fence_pending:
            fence_ops, fence_streams = self.fence_pending.pop(ph)
            deps.update(fence_ops)
            for s, c in fence_streams.items():
                o.dwaits[s] = max(o.dwaits.get(s, 0), c)
        for p in deps:
            if p.dma:
                if o.stream is not None and p.stream == o.stream:
                    continue
                c = self.stream_cnt[p.stream]
                o.dwaits[p.stream] = max(o.dwaits.get(p.stream, 0), c)
            elif ph == "tensor" and PHYS[p.veng] == "tensor":
                pass
            else:
                o.deps.add(p)
                p.need_inc = True
        if dma:
            self.stream_cnt[stream] = self.stream_cnt.get(stream, 0) + 1
        if force_inc:
            o.need_inc = True
        for t in writes:
            st = self.acc.setdefault(t, {"w": {}, "r": []})
            st["w"] = {ph: o}
            st["r"] = []
        for t in reads:
            st = self.acc.setdefault(t, {"w": {}, "r": []})
            st["r"].append(o)
        self.phys_ops[ph].append(o)
        self.n_ops += 1
        return o

    def barrier(self):
        lasts = []
        for ph, ops in self.phys_ops.items():
            for o in reversed(ops):
                if not o.dma:
                    o.need_inc = True
                    lasts.append(o)
                    break
        snap = dict(self.stream_cnt)
        for ph in self.phys_ops:
            self.fence_pending[ph] = (set(lasts), snap)
        self.acc = {}

    def emit(self, nc, block, sems, dma_sems):
        cnt = {v: 0 for v in VENGS}
        for ph, ops in self.phys_ops.items():
            for o in ops:
                if o.need_inc:
                    cnt[o.veng] += 1
                    o.val = cnt[o.veng]

        def run_stream(eng, ph):
            guar = {}
            def wait(sem, val):
                k = id(sem)
                if guar.get(k, -1) < val:
                    eng.wait_ge(sem, val)
                    guar[k] = val
            for o in self.phys_ops[ph]:
                waits = {}
                for p in o.deps:
                    assert p.val is not None, "dep without inc"
                    waits[p.veng] = max(waits.get(p.veng, 0), p.val)
                for v, val in sorted(waits.items()):
                    wait(sems[v], val)
                for s, c in sorted(o.dwaits.items()):
                    wait(dma_sems[s], 16 * c)
                inst = o.fn(eng)
                if o.dma:
                    inst.then_inc(dma_sems[o.stream], 16)
                elif o.need_inc:
                    inst.then_inc(sems[o.veng], 1)

        @block.sync
        def _(sync):
            run_stream(sync, "sync")

        @block.tensor
        def _(tensor):
            run_stream(tensor, "tensor")

        @block.vector
        def _(vector):
            run_stream(vector, "vector")

        @block.scalar
        def _(scalar):
            run_stream(scalar, "scalar")

        @block.gpsimd
        def _(gpsimd):
            run_stream(gpsimd, "gpsimd")


def host_constants():
    d = {}
    inv = 1.0 / (10000.0 ** (np.arange(0, DH, 2, dtype=np.float64) / DH))
    t = np.arange(T, dtype=np.float64)
    f = t[:, None] * inv[None, :]
    d["cosT"] = np.cos(f).T.astype(bfd).copy()
    d["sinT"] = np.sin(f).T.astype(bfd).copy()
    d["ident"] = np.eye(128, dtype=bfd)
    trig = np.array([[sx, sy, sz] for sx in (1.0, -1.0) for sy in (1.0, -1.0)
                     for sz in (1.0, -1.0)], dtype=np.float64) / np.sqrt(3.0)
    dirs = np.zeros((128, 8), dtype=bfd)
    dirs[0:3, :] = trig.T.astype(bfd)
    d["dirs"] = dirs
    d["ones1"] = np.ones((1, 128), dtype=bfd)
    s = np.arange(512)[None, :]
    p = np.arange(128)[:, None]
    m = np.concatenate([np.where(s <= 128 * j + p, 0.0, NEG) for j in range(4)],
                       axis=1)
    d["maskS"] = m.astype(np.float32)
    d["maskG"] = m.astype(bfd)
    nb = np.empty((128, 2), dtype=np.float32)
    nb[:, 0] = -16.0
    nb[:, 1] = -20.0
    d["negb"] = nb
    return d


_CONSTS = host_constants()
_CACHED = {}


P3_SERIAL = False


def build_program(debug_outs=()):
    nc = bass.Bass("TRN2", target_bir_lowering=False, debug=False)
    ap = {}
    def inp(name, shape, dt):
        ap[name] = nc.dram_tensor(name, shape, dt, kind="ExternalInput").ap()
    inp("X", [T, C], F32)
    inp("WQ", [C, C], F32)
    inp("WK", [C, KVH * DH], F32)
    inp("WV", [C, KVH * DH], F32)
    inp("WO", [C, C], F32)
    inp("WF", [C, C], F32)
    inp("BIAS", [1, C], F32)
    inp("HS", [128, H], F32)
    inp("cosT", [128, T], BF16)
    inp("sinT", [128, T], BF16)
    inp("ident", [128, 128], BF16)
    inp("dirs", [128, H], BF16)
    inp("ones1", [1, 128], BF16)
    inp("maskS", [128, 2048], F32)
    inp("maskG", [128, 2048], BF16)
    inp("negb", [128, 2], F32)
    ap["OUT"] = nc.dram_tensor("OUT", [T, C], F32, kind="ExternalOutput").ap()
    def scratch(name, shape):
        kind = "ExternalOutput" if name in debug_outs else "Internal"
        ap[name] = nc.dram_tensor(name, shape, BF16, kind=kind).ap()
    scratch("xn_dram", [T, C])
    scratch("xnt2", [NT, 128, NCH, 128])
    scratch("qf_dram", [C, C])
    scratch("qfT_dram", [NCH, 128, NCH, 128])
    scratch("q_dram", [4, NCH, 128, 512])
    scratch("kpb_dram", [H, 128, T])
    scratch("v_dram", [KVH * DH, T])
    scratch("ao2", [NT, 128, NCH, 128])
    scratch("vT_dram", [T, KVH * DH])
    if "dbg_stats" in debug_outs:
        ap["dbg_stats"] = nc.dram_tensor("dbg_stats", [NT, 128, 16], F32,
                                         kind="ExternalOutput").ap()
    if "dbg_kt" in debug_outs:
        ap["dbg_kt"] = nc.dram_tensor("dbg_kt", [128, 4 * T], BF16,
                                      kind="ExternalOutput").ap()
        ap["dbg_qp"] = nc.dram_tensor("dbg_qp", [128, 128], BF16,
                                      kind="ExternalOutput").ap()
    if "dbg_vtok" in debug_outs:
        ap["dbg_vtok"] = nc.dram_tensor("dbg_vtok", [128, NT * 512], BF16,
                                        kind="ExternalOutput").ap()
        ap["dbg_vtok2"] = nc.dram_tensor("dbg_vtok2", [128, NT * 512], BF16,
                                         kind="ExternalOutput").ap()

    rec = Rec()

    from contextlib import ExitStack
    with ExitStack() as _ctx:
        ec = _ctx.enter_context
        sb_id = ec(nc.sbuf_tensor("sb_id", [128, 128], BF16))
        sb_ones = ec(nc.sbuf_tensor("sb_ones", [1, 128], BF16))
        sb_dirs = ec(nc.sbuf_tensor("sb_dirs", [128, H], BF16))
        sb_hs = ec(nc.sbuf_tensor("sb_hs", [128, H], F32))
        sb_hs2 = ec(nc.sbuf_tensor("sb_hs2", [128, 2 * H], F32))
        sb_negb = ec(nc.sbuf_tensor("sb_negb", [128, 2], F32))
        sb_sums = ec(nc.sbuf_tensor("sb_sums", [128, 16], F32))
        sb_qp = ec(nc.sbuf_tensor("sb_qp", [128, 128], F32))
        sb_bias = ec(nc.sbuf_tensor("sb_bias", [1, C], BF16))
        sb_maskS = ec(nc.sbuf_tensor("sb_maskS", [128, 2048], F32))
        sb_maskG = ec(nc.sbuf_tensor("sb_maskG", [128, 2048], BF16))
        sb_small = ec(nc.sbuf_tensor("sb_small", [128, 64], F32))
        sb_kprow = ec(nc.sbuf_tensor("sb_kprow", [1, T], BF16))
        sb_cos = ec(nc.sbuf_tensor("sb_cos", [128, T], BF16))
        sb_sin = ec(nc.sbuf_tensor("sb_sin", [128, T], BF16))
        kT = ec(nc.sbuf_tensor("kT", [128, 4 * T], BF16))
        vtok = ec(nc.sbuf_tensor("vtok", [128, NT * 512], BF16))
        AR16 = ec(nc.sbuf_tensor("AR16", [128, 54272], BF16))
        AR32 = ec(nc.sbuf_tensor("AR32", [128, 10240], F32))
        ps_s = ec(nc.psum_tensor("ps_s", [128, 2048], F32))
        ps_pt0 = ec(nc.psum_tensor("ps_pt0", [128, 512], F32))
        ps_pt1 = ec(nc.psum_tensor("ps_pt1", [128, 512], F32))
        ps_ov0 = ec(nc.psum_tensor("ps_ov0", [128, 512], F32))
        ps_ov1 = ec(nc.psum_tensor("ps_ov1", [128, 512], F32))
        s_sync = ec(nc.semaphore("s_sync"))
        s_tensor = ec(nc.semaphore("s_tensor"))
        s_vector = ec(nc.semaphore("s_vector"))
        s_scalar = ec(nc.semaphore("s_scalar"))
        s_gp = ec(nc.semaphore("s_gp"))
        s_gpd = ec(nc.semaphore("s_gpd"))
        s_scd = ec(nc.semaphore("s_scd"))
        block = ec(nc.Block())
        sems = {"sync": s_sync, "tensor": s_tensor, "vector": s_vector,
                "scalar": s_scalar, "gp": s_gp, "gpd": s_gpd, "scd": s_scd}

        def a16(lo, n):
            return AR16[:, lo:lo + n]

        def a32(lo, n):
            return AR32[:, lo:lo + n]

        small_slot = {}

        def slot(name):
            if name not in small_slot:
                small_slot[name] = len(small_slot)
            i = small_slot[name]
            assert i < 64
            return sb_small[:, i:i + 1]

        def dma(dst, src, reads, writes, q="sync", force_inc=False, stream=None):
            rec.op("gpd" if q == "gpd" else "sync",
                   lambda e, d=dst, s=src: e.dma_start(out=d, in_=s),
                   reads=reads, writes=writes, dma=True,
                   stream=stream or writes[0])

        def dmat(dst, src, reads, writes, stream=None, q="sync"):
            rec.op(q,
                   lambda e, d=dst, s=src: e.dma_start(out=d, in_=s, transpose=True),
                   reads=reads, writes=writes, dma=True,
                   stream=stream or writes[0])

        def mm(out, lhsT, rhs, start, stop, reads, writes, label=None):
            rec.op("tensor",
                   lambda e, o=out, l=lhsT, r=rhs, st=start, sp=stop:
                       e.matmul(o, l, r, start=st, stop=sp),
                   reads=reads, writes=writes, label=label)

        def vop(fn, reads, writes):
            rec.op("vector", fn, reads=reads, writes=writes)

        def sop(fn, reads, writes):
            rec.op("scalar", fn, reads=reads, writes=writes)

        dma(sb_id[:], ap["ident"][:], ["ident"], ["sb_id"], stream="init")
        dma(sb_dirs[:], ap["dirs"][:], ["dirs"], ["sb_dirs"], stream="init")
        dma(sb_ones[:], ap["ones1"][:], ["ones1"], ["sb_ones"], stream="init")
        dma(sb_hs[:], ap["HS"][:], ["HS"], ["sb_hs"], stream="init")
        dma(sb_maskS[:], ap["maskS"][:], ["maskS"], ["sb_maskS"], stream="init")
        dma(sb_maskG[:], ap["maskG"][:], ["maskG"], ["sb_maskG"], stream="init")
        dma(sb_cos[:], ap["cosT"][:], ["cosT"], ["sb_cos"], stream="init")
        dma(sb_sin[:], ap["sinT"][:], ["sinT"], ["sb_sin"], stream="init")
        dma(sb_bias[:], ap["BIAS"][:], ["BIAS"], ["sb_bias"], q="gpd")
        dma(sb_negb[:], ap["negb"][:], ["negb"], ["sb_negb"], stream="init")
        hs2v = sb_hs2[:].rearrange("p (a b) -> p a b", b=2)
        vop(lambda e: e.tensor_scalar(hs2v[:, :, 0], sb_hs[:], -1.0, 1.0,
                                      ALU.mult, ALU.add),
            ["sb_hs"], ["sb_hs2"])
        vop(lambda e: e.tensor_copy(hs2v[:, :, 1], sb_hs[:]),
            ["sb_hs"], ["sb_hs2"])

        WK, WV = 32768, 40960
        WQB = [16384, 24576]

        WQA = {0: WQB[0], 1: WQB[1], 2: WK, 3: WV}

        def cast_quarter(hp, emit=True):
            extra = [] if hp < 2 else (["wk"] if hp == 2 else ["wv"])
            ths = []
            for cch in range(NCH):
                def th(cch=cch, hp=hp, extra=extra):
                    dma(a16(WQA[hp] + cch * 512, 512),
                        ap["WQ"][cch * 128:(cch + 1) * 128,
                                 hp * 512:(hp + 1) * 512],
                        ["WQ"], extra + [f"wq{hp}"], q="gpd",
                        stream=f"wq{hp}")
                if emit:
                    th()
                else:
                    ths.append(th)
            return ths

        pending_casts = list(cast_quarter(0, emit=False))
        for cch in range(NCH):
            def thk(cch=cch):
                dma(a16(WK + cch * 512, 512),
                    ap["WK"][cch * 128:(cch + 1) * 128, :],
                    ["WK"], ["wk"], q="gpd")
            pending_casts.append(thk)
        for cch in range(NCH):
            def thv(cch=cch):
                dma(a16(WV + cch * 512, 512),
                    ap["WV"][cch * 128:(cch + 1) * 128, :],
                    ["WV"], ["wv"], q="gpd")
            pending_casts.append(thv)
        pending_casts += cast_quarter(1, emit=False)

        X_TILE = [0, 2048, 4096, 6144]
        WF_TILE = [8192]
        XN_BF = [0, 2048, 4096, 6144]
        GTP = [8192, 10240]
        LTN = [12288, 14336]

        def p1_stats(grp4):
            gb = grp4 % 2
            st = f"st{gb}"
            ssq4 = sb_small[:, gb * 32:gb * 32 + 4]
            sm4 = sb_small[:, gb * 32 + 4:gb * 32 + 8]
            mean4 = sb_small[:, gb * 32 + 8:gb * 32 + 12]
            msq4 = sb_small[:, gb * 32 + 12:gb * 32 + 16]
            varn4 = sb_small[:, gb * 32 + 16:gb * 32 + 20]
            rstd4 = sb_small[:, gb * 32 + 20:gb * 32 + 24]
            negmr4 = sb_small[:, gb * 32 + 24:gb * 32 + 28]
            vop(lambda e, o=mean4, x=sm4: e.tensor_scalar(o, x, 1.0 / C, None,
                                                          ALU.mult), [st], [st])
            vop(lambda e, o=msq4, x=mean4: e.tensor_tensor(o, x, x, ALU.mult),
                [st], [st])
            vop(lambda e, o=varn4, x=ssq4: e.tensor_scalar(o, x, 1.0 / C, None,
                                                           ALU.mult), [st], [st])
            vop(lambda e, o=varn4, x=msq4: e.tensor_tensor(o, o, x, ALU.subtract),
                [st], [st])
            vop(lambda e, o=varn4: e.tensor_scalar(o, o, 1e-5, None, ALU.add),
                [st], [st])
            sop(lambda e, o=rstd4, x=varn4: e.activation(o, x, AF.Sqrt),
                [st], [st])
            vop(lambda e, o=rstd4: e.reciprocal(o, o), [st], [st])
            vop(lambda e, o=negmr4, m=mean4, r=rstd4:
                    e.tensor_tensor(o, m, r, ALU.mult), [st], [st])
            vop(lambda e, o=negmr4: e.tensor_scalar(o, o, -1.0, None, ALU.mult),
                [st], [st])
            return rstd4, negmr4

        for i in range(NT):
            grp4, ii = i // 4, i % 4
            gb = grp4 % 2
            st = f"st{gb}"
            xb = i % 4
            xt = a32(X_TILE[xb], 2048)
            xt_t = f"xt{xb}"
            dma(xt, ap["X"][i * 128:(i + 1) * 128, :], ["X"], [xt_t])
            xnb = a16(XN_BF[xb], 2048)
            xnb_t = f"xnb{xb}"
            ssqc = sb_small[:, gb * 32 + ii:gb * 32 + ii + 1]
            smc = sb_small[:, gb * 32 + 4 + ii:gb * 32 + 4 + ii + 1]
            sop(lambda e, o=xnb, x=xt, a=ssqc: e.activation(o, x, AF.Square,
                                                            accum_out=a),
                [xt_t], [xnb_t, st])
            sop(lambda e, o=xnb, x=xt, a=smc: e.activation(o, x, AF.Identity,
                                                           accum_out=a),
                [xt_t], [xnb_t, st])
            if ii == 3:
                rstd4, negmr4 = p1_stats(grp4)
                for jj in range(4):
                    i2 = grp4 * 4 + jj
                    xb2 = i2 % 4
                    xt2 = a32(X_TILE[xb2], 2048)
                    xnb2 = a16(XN_BF[xb2], 2048)
                    xnb2_t = f"xnb{xb2}"
                    sop(lambda e, o=xnb2, x=xt2, r=rstd4[:, jj:jj + 1],
                        b=negmr4[:, jj:jj + 1]:
                            e.activation(o, x, AF.Identity, bias=b, scale=r),
                        [f"xt{xb2}", st], [xnb2_t])
                    xsg = vtok[:, (jj % 2) * 2048:(jj % 2) * 2048 + 2048]
                    xsg_t = f"xstg{jj % 2}"
                    for grp in range(4):
                        psb = [ps_pt0, ps_pt1][grp % 2]
                        pst = f"p1pt{grp % 2}"
                        for cc in range(4):
                            cch = grp * 4 + cc
                            mm(psb[:, cc * 128:(cc + 1) * 128],
                               xnb2[:, cch * 128:(cch + 1) * 128], sb_id[:],
                               True, True, [xnb2_t, "sb_id"], [pst])
                        ev = xsg[:, grp * 512:(grp + 1) * 512]
                        if grp % 2 == 0:
                            sop(lambda e, o=ev, x=psb[:]: e.copy(o, x),
                                [pst], [xsg_t])
                        else:
                            vop(lambda e, o=ev, x=psb[:]: e.tensor_copy(o, x),
                                [pst], [xsg_t])
                    dma(ap["xnt2"][i2], xsg, [xsg_t], ["xnt2"], q="gpd", stream=xsg_t)
            wt = a32(WF_TILE[0], 2048)
            wt_t = "wt0"
            dma(wt, ap["WF"][i * 128:(i + 1) * 128, :], ["WF"], [wt_t])
            wb = i % 2
            ths = sb_small[:, 60:61]
            thr = sb_small[:, 61:62]
            nthr = sb_small[:, 62:63]
            vop(lambda e, o=ths, x=wt: e.reduce_sum(o, x, AX.X,
                                                    apply_absolute_value=True),
                [wt_t], ["ths"])
            vop(lambda e, o=thr, x=ths: e.tensor_scalar(o, x, 0.7 / C, None, ALU.mult),
                ["ths"], ["thr"])
            vop(lambda e, o=nthr, x=thr: e.tensor_scalar(o, x, -1.0, None, ALU.mult),
                ["thr"], ["nthr"])
            gtp = a16(GTP[wb], 2048)
            ltn = a16(LTN[wb], 2048)
            vop(lambda e, o=gtp, x=wt, t=thr: e.tensor_scalar(o, x, t, None, ALU.is_gt),
                [wt_t, "thr"], [f"gtp{wb}"])
            vop(lambda e, o=ltn, x=wt, t=nthr: e.tensor_scalar(o, x, t, None, ALU.is_lt),
                [wt_t, "nthr"], [f"ltn{wb}"])
            vop(lambda e, o=gtp, x=ltn: e.tensor_tensor(o, o, x, ALU.subtract),
                [f"gtp{wb}", f"ltn{wb}"], [f"gtp{wb}"])
            qsg = vtok[:, 4096 + wb * 2048:4096 + wb * 2048 + 2048]
            qsg_t = f"qstg{wb}"
            for grp in range(4):
                psb = [ps_pt0, ps_pt1][grp % 2]
                pst = f"p1pt{grp % 2}"
                for cc in range(4):
                    ich = grp * 4 + cc
                    mm(psb[:, cc * 128:(cc + 1) * 128],
                       gtp[:, ich * 128:(ich + 1) * 128], sb_id[:],
                       True, True, [f"gtp{wb}", "sb_id"], [pst])
                ev = qsg[:, grp * 512:(grp + 1) * 512]
                if grp % 2 == 0:
                    sop(lambda e, o=ev, x=psb[:]: e.copy(o, x), [pst], [qsg_t])
                else:
                    vop(lambda e, o=ev, x=psb[:]: e.tensor_copy(o, x), [pst], [qsg_t])
            dma(ap["qfT_dram"][i], qsg, [qsg_t], ["qfT_dram"], q="gpd", stream=qsg_t)
            for th in pending_casts[i * 4:i * 4 + 4]:
                th()
        rec.barrier()

        XNT = [0, 8192]
        WRK = 49152
        RW = {}
        for par in range(2):
            base = WRK + par * 2560
            RW[par] = dict(
                qa_raw=a16(base + 0, 512), qb_raw=a16(base + 512, 512),
                qa2=a16(base + 1024, 512), qb2=a16(base + 1536, 512),
                tr1=a16(base + 2048, 512))

        qp_pending = []
        for hpp in range(2):
            if hpp == 1:
                cast_quarter(2)
                cast_quarter(3)
            for tt5 in range(4):
                xb = (hpp * 4 + tt5) % 2
                xnt = a16(XNT[xb], 8192)
                xnt_t = f"xnt{xb}"
                xnt4 = xnt.rearrange("p (c gg t) -> p c gg t", c=16, gg=4)
                for gg in range(4):
                    dma(xnt4[:, :, gg, :], ap["xnt2"][tt5 * 4 + gg],
                        ["xnt2"], [xnt_t], q="sync" if gg % 2 == 0 else "scd")
                cs = sb_cos[:, tt5 * 512:(tt5 + 1) * 512]
                sn = sb_sin[:, tt5 * 512:(tt5 + 1) * 512]

                def rope_pair(par, psA, psB, tA, tB, outA, outA_t, outB, outB_t,
                              cs=cs, sn=sn):
                    w = RW[par]
                    qa_raw, qb_raw = w["qa_raw"], w["qb_raw"]
                    qa2, qb2, tr1 = w["qa2"], w["qb2"], w["tr1"]
                    ta, tb = f"qa_raw{par}", f"qb_raw{par}"
                    t2a, t2b, ttr = f"qa2_{par}", f"qb2_{par}", f"tr1_{par}"
                    sop(lambda e, o=qa_raw, x=psA: e.copy(o, x), [tA], [ta])
                    sop(lambda e, o=qb_raw, x=psB: e.copy(o, x), [tB], [tb])
                    vop(lambda e, o=qa2, x=qa_raw, c=cs: e.tensor_tensor(o, x, c, ALU.mult),
                        [ta, "sb_cos"], [t2a])
                    vop(lambda e, o=tr1, x=qb_raw, c=sn: e.tensor_tensor(o, x, c, ALU.mult),
                        [tb, "sb_sin"], [ttr])
                    vop(lambda e, o=outA, x=qa2, y=tr1: e.tensor_tensor(o, x, y, ALU.subtract),
                        [t2a, ttr], [outA_t])
                    vop(lambda e, o=qb2, x=qb_raw, c=cs: e.tensor_tensor(o, x, c, ALU.mult),
                        [tb, "sb_cos"], [t2b])
                    vop(lambda e, o=tr1, x=qa_raw, c=sn: e.tensor_tensor(o, x, c, ALU.mult),
                        [ta, "sb_sin"], [ttr])
                    vop(lambda e, o=outB, x=qb2, y=tr1: e.tensor_tensor(o, x, y, ALU.add),
                        [t2b, ttr], [outB_t])

                for hh4 in range(4):
                    hp = 2 * hpp + hh4 // 2
                    hh = hh4 % 2
                    h = 2 * hp + hh
                    par = hh
                    psA = ps_s[:, par * 1024:par * 1024 + 512]
                    psB = ps_s[:, par * 1024 + 512:par * 1024 + 1024]
                    tA, tB = f"psq{par}a", f"psq{par}b"
                    for cch in range(NCH):
                        lA = a16(WQA[hp] + cch * 512 + hh * 256, 128)
                        lB = a16(WQA[hp] + cch * 512 + hh * 256 + 128, 128)
                        r = a16(XNT[xb] + cch * 512, 512)
                        mm(psA, lA, r, cch == 0, cch == NCH - 1,
                           [f"wq{hp}", xnt_t], [tA])
                        mm(psB, lB, r, cch == 0, cch == NCH - 1,
                           [f"wq{hp}", xnt_t], [tB])
                    for th in qp_pending:
                        th()
                    qp_pending = []
                    qa2, qb2 = RW[par]["qa2"], RW[par]["qb2"]
                    t2a, t2b = f"qa2_{par}", f"qb2_{par}"
                    rope_pair(par, psA, psB, tA, tB, qa2, t2a, qb2, t2b)
                    dma(ap["q_dram"][tt5, 2 * h], qa2, [t2a], ["q_dram"], q="gpd", stream=t2a)
                    dma(ap["q_dram"][tt5, 2 * h + 1], qb2, [t2b], ["q_dram"], q="gpd", stream=t2b)

                    def qp_thunk(h=h, hh=hh, tt5=tt5, qa2=qa2, t2a=t2a):
                        for j in range(4):
                            mm(ps_pt0[:, hh * 4 + j:hh * 4 + j + 1],
                               qa2[:, j * 128:(j + 1) * 128],
                               sb_dirs[:, h:h + 1], True, True,
                               [t2a, "sb_dirs"], ["psqp"])
                        vop(lambda e,
                            o=sb_qp[:, h * 16 + tt5 * 4:h * 16 + tt5 * 4 + 4],
                            x=ps_pt0[:, hh * 4:hh * 4 + 4]: e.tensor_copy(o, x),
                            ["psqp"], ["sb_qp"])
                    qp_pending = [qp_thunk]
                for th in qp_pending:
                    th()
                qp_pending = []
                if hpp == 0:
                    for kvp in range(KVH):
                        par = kvp % 2
                        psA = ps_s[:, par * 1024:par * 1024 + 512]
                        psB = ps_s[:, par * 1024 + 512:par * 1024 + 1024]
                        tA, tB = f"psq{par}a", f"psq{par}b"
                        for cch in range(NCH):
                            lA = a16(WK + cch * 512 + kvp * 256, 128)
                            lB = a16(WK + cch * 512 + kvp * 256 + 128, 128)
                            r = a16(XNT[xb] + cch * 512, 512)
                            mm(psA, lA, r, cch == 0, cch == NCH - 1,
                               ["wk", xnt_t], [tA])
                            mm(psB, lB, r, cch == 0, cch == NCH - 1,
                               ["wk", xnt_t], [tB])
                        oA = kT[:, (2 * kvp) * T + tt5 * 512:(2 * kvp) * T + (tt5 + 1) * 512]
                        oB = kT[:, (2 * kvp + 1) * T + tt5 * 512:(2 * kvp + 1) * T + (tt5 + 1) * 512]
                        rope_pair(par, psA, psB, tA, tB, oA, "kT", oB, "kT")
                    for vf in range(4):
                        for cch in range(NCH):
                            l = a16(WV + cch * 512 + vf * 128, 128)
                            r = a16(XNT[xb] + cch * 512, 512)
                            mm(ps_ov1[:], l, r, cch == 0, cch == NCH - 1,
                               ["wv", xnt_t], ["psv"])
                        vs = vtok[:, vf * 512:(vf + 1) * 512]
                        vs_t = f"vs4_{vf}"
                        sop(lambda e, o=vs, x=ps_ov1[:]: e.copy(o, x),
                            ["psv"], [vs_t])
                    for tb4 in range(4):
                        for vf in range(4):
                            mm(ps_pt1[:, vf * 128:(vf + 1) * 128],
                               vtok[:, vf * 512 + tb4 * 128:
                                     vf * 512 + (tb4 + 1) * 128],
                               sb_id[:], True, True,
                               [f"vs4_{vf}", "sb_id"], ["p2vt"])
                        vts = vtok[:, 2048 + (tb4 % 2) * 512:
                                   2048 + (tb4 % 2) * 512 + 512]
                        vts_t = f"vts{tb4 % 2}"
                        if tb4 % 2 == 0:
                            sop(lambda e, o=vts: e.copy(o, ps_pt1[:]),
                                ["p2vt"], [vts_t])
                        else:
                            vop(lambda e, o=vts: e.tensor_copy(o, ps_pt1[:]),
                                ["p2vt"], [vts_t])
                        dma(ap["vT_dram"][tt5 * 512 + tb4 * 128:
                                          tt5 * 512 + (tb4 + 1) * 128, :],
                            vts, [vts_t], ["vT_dram"], q="gpd", stream=vts_t)
        for th in qp_pending:
            th()
        qp_pending = []
        rec.barrier()
        kpb_sts = [a16(0, 2048), a16(2048, 2048)]
        for h in range(H):
            kpb_st = kpb_sts[h % 2]
            kpb_st_t = f"kpb_st{h % 2}"
            kvc = 2 * (h // 4)
            for tt5 in range(4):
                mm(ps_pt1[0:1, 0:512], sb_dirs[:, h:h + 1],
                   kT[:, kvc * T + tt5 * 512:kvc * T + (tt5 + 1) * 512],
                   True, True, ["sb_dirs", "kT"], ["pskp"])
                vop(lambda e, o=sb_kprow[0:1, tt5 * 512:(tt5 + 1) * 512],
                    x=ps_pt1[0:1, 0:512]: e.tensor_copy(o, x),
                    ["pskp"], ["kprow"])
            for tt5 in range(4):
                mm(ps_ov0[:], sb_ones[:], sb_kprow[0:1, tt5 * 512:(tt5 + 1) * 512],
                   True, True, ["sb_ones", "kprow"], ["pskpb"])
                sop(lambda e, o=kpb_st[:, tt5 * 512:(tt5 + 1) * 512], x=ps_ov0[:]:
                        e.copy(o, x), ["pskpb"], [kpb_st_t])
            dma(ap["kpb_dram"][h], kpb_st, [kpb_st_t], ["kpb_dram"], q="gpd", stream=kpb_st_t)
        rec.barrier()

        if "dbg_kt" in ap:
            dma(ap["dbg_kt"][:], kT[:], ["kT"], ["dbg_kt"], q="gpd")
            dma(ap["dbg_qp"][:], sb_qp[:], ["sb_qp"], ["dbg_qp"], q="gpd")
            rec.barrier()
        PC = 0
        PTG = 16384
        QG = 24576
        KPB = [32768, 34816]
        EXPS = [36864, 38912]
        RAWG = [40960, 43008]
        EXPG = [45056, 47104]
        AOST = [49152, 50176]

        for tc in range(NT):
            dma(vtok[:, tc * 512:(tc + 1) * 512],
                ap["vT_dram"][tc * 128:(tc + 1) * 128, :],
                ["vT_dram"], ["vtok"])

        if "dbg_vtok" in ap:
            rec.barrier()
            dma(ap["dbg_vtok"][:], vtok[:], ["vtok"], ["dbg_vtok"], q="gpd")
            rec.barrier()

        def softmax_A(g, h, j):
            qt = 4 * g + j
            sset = h % 2
            jb = j % 2
            ndw = g * 512
            klen = (g + 1) * 512
            qg_lA = a16(QG + (2 * h) * 512 + j * 128, 128)
            qg_lB = a16(QG + (2 * h + 1) * 512 + j * 128, 128)
            kvc = 2 * (h // 4)
            exps = a16(EXPS[jb], 2048)
            exps_t = f"expS{jb}"
            sums = sb_sums[:, 8 * jb:8 * jb + 8]
            sums_t = f"sums{jb}"
            sbase = jb * 2 if g <= 1 else 0
            for kt in range(g + 1):
                rhsA = kT[:, kvc * T + kt * 512:kvc * T + (kt + 1) * 512]
                rhsB = kT[:, (kvc + 1) * T + kt * 512:(kvc + 1) * T + (kt + 1) * 512]
                o = ps_s[:, (sbase + kt) * 512:(sbase + kt + 1) * 512]
                bt = "ps_s%d" % (sbase + kt)
                mm(o, qg_lA, rhsA, True, False, ["q_g", "kT"], [bt])
                mm(o, qg_lB, rhsB, False, True, ["q_g", "kT"], [bt])
                if kt == g:
                    vop(lambda e, o=o: e.tensor_tensor(
                            o, o, sb_maskS[:, j * 512:(j + 1) * 512], ALU.add),
                        [bt, "sb_maskS"], [bt])
                sop(lambda e, o=exps[:, kt * 512:(kt + 1) * 512], x=o,
                    a=sums[:, kt:kt + 1]:
                        e.activation(o, x, AF.Exp, bias=sb_negb[:, 0:1],
                                     scale=SCALE, accum_out=a),
                    [bt, "sb_negb"], [exps_t, sums_t])
            rg = a16(RAWG[jb], 2048)
            rg_t = f"rawG{jb}"
            qp_col = sb_qp[:, h * 16 + qt:h * 16 + qt + 1]
            kpb = a16(KPB[sset], 2048)
            vop(lambda e, o=rg[:, 0:klen], x=kpb[:, 0:klen], s=qp_col:
                    e.tensor_scalar(o, x, s, None, ALU.mult),
                [f"kpb{sset}", "sb_qp"], [rg_t])
            vop(lambda e, o=rg[:, ndw:klen]:
                    e.tensor_tensor(o, o, sb_maskG[:, j * 512:(j + 1) * 512],
                                    ALU.add),
                [rg_t, "sb_maskG"], [rg_t])
            expg = a16(EXPG[jb], 2048)
            expg_t = f"expG{jb}"
            sop(lambda e, o=expg[:, 0:klen], x=rg[:, 0:klen], a=sums[:, 5:6]:
                    e.activation(o, x, AF.Exp, bias=sb_negb[:, 1:2], accum_out=a),
                [rg_t, "sb_negb"], [expg_t, sums_t])
        def softmax_B(g, h, j):
            qt = 4 * g + j
            sset = h % 2
            jb = j % 2
            klen = (g + 1) * 512
            exps = a16(EXPS[jb], 2048)
            exps_t = f"expS{jb}"
            expg = a16(EXPG[jb], 2048)
            expg_t = f"expG{jb}"
            sums = sb_sums[:, 8 * jb:8 * jb + 8]
            sums_t = f"sums{jb}"
            if g == 0:
                vop(lambda e, o=sums[:, 4:5], x=sums[:, 0:1]: e.tensor_copy(o, x),
                    [sums_t], [sums_t])
            else:
                vop(lambda e, o=sums[:, 4:5], x=sums[:, 0:g + 1], gg=g:
                        e.reduce_sum(o, x, AX.X),
                    [sums_t], [sums_t])
            vop(lambda e, o=sums[:, 6:8], x=sums[:, 4:6]: e.reciprocal(o, x),
                [sums_t], [sums_t])
            vop(lambda e, o=sums[:, 6:8], x=sb_hs2[:, 2 * h:2 * h + 2]:
                    e.tensor_tensor(o, o, x, ALU.mult),
                [sums_t, "sb_hs2"], [sums_t])
            vop(lambda e, o=expg[:, 0:klen], s=sums[:, 7:8]:
                    e.tensor_scalar(o, o, s, None, ALU.mult),
                [expg_t, sums_t], [expg_t])
            pc = a16(PC + (sset * 4 + j) * 2048, 2048)
            vop(lambda e, o=exps[:, 0:klen], s=sums[:, 6:7]:
                    e.tensor_scalar(o, o, s, None, ALU.mult),
                [exps_t, sums_t], [exps_t])
            vop(lambda e, o=pc[:, 0:klen], x=exps[:, 0:klen], y=expg[:, 0:klen]:
                    e.tensor_tensor(o, x, y, ALU.add),
                [exps_t, expg_t], [f"pc{sset}_{j}"])

        def ptov_units(g, h):
            sset = h % 2
            kvc4 = (h // 4) * 256
            nkc = 4 * g + 4
            units = []

            def memsets():
                for j0 in range(1, 4):
                    kc = 4 * g + j0
                    o = a16(PTG + kc * 512, j0 * 128)
                    vop(lambda e, o=o: e.memset(o, 0.0), [], ["ptg"])
            units.append(memsets)

            def tp_unit(kc):
                j0 = max(0, kc - 4 * g)
                pst = [ps_pt0, ps_pt1][kc % 2]
                pst_t = f"ps_pt{kc % 2}"
                for j in range(j0, 4):
                    pc = a16(PC + (sset * 4 + j) * 2048, 2048)
                    mm(pst[:, j * 128:(j + 1) * 128],
                       pc[:, kc * 128:(kc + 1) * 128], sb_id[:],
                       True, True, [f"pc{sset}_{j}", "sb_id"], [pst_t])
                ev = a16(PTG + kc * 512 + j0 * 128, 512 - j0 * 128)
                if kc % 2 == 0:
                    sop(lambda e, o=ev, x=pst[:, j0 * 128:512]: e.copy(o, x),
                        [pst_t], ["ptg"])
                else:
                    vop(lambda e, o=ev, x=pst[:, j0 * 128:512]: e.tensor_copy(o, x),
                        [pst_t], ["ptg"])

            def ov_unit(kc):
                lhs0 = vtok[:, kc * 512 + kvc4:kc * 512 + kvc4 + 128]
                lhs1 = vtok[:, kc * 512 + kvc4 + 128:kc * 512 + kvc4 + 256]
                rhs = a16(PTG + kc * 512, 512)
                mm(ps_ov0[:], lhs0, rhs, kc == 0, kc == nkc - 1,
                   ["vtok", "ptg"], ["ps_ov0"])
                mm(ps_ov1[:], lhs1, rhs, kc == 0, kc == nkc - 1,
                   ["vtok", "ptg"], ["ps_ov1"])

            LAG = 4
            seq = []
            for kc in range(nkc):
                seq.append(("tp", kc))
                if kc >= LAG:
                    seq.append(("ov", kc - LAG))
            for kc in range(max(0, nkc - LAG), nkc):
                seq.append(("ov", kc))
            for kind, kc in seq:
                def unit(kind=kind, kc=kc):
                    (tp_unit if kind == "tp" else ov_unit)(kc)
                units.append(unit)

            def tail():
                ab = h % 2
                ao = a16(AOST[ab], 1024)
                ao_t = f"ao{ab}"
                sop(lambda e, o=ao[:, 0:512], x=ps_ov0[:]: e.copy(o, x),
                    ["ps_ov0"], [ao_t])
                sop(lambda e, o=ao[:, 512:1024], x=ps_ov1[:]: e.copy(o, x),
                    ["ps_ov1"], [ao_t])
                for dch in range(2):
                    src3 = ao[:, dch * 512:(dch + 1) * 512].rearrange(
                        "p (g2 t) -> p g2 t", g2=4)
                    dma(ap["ao2"][4 * g:4 * g + 4, :, 2 * h + dch]
                        .rearrange("g2 p t -> p g2 t"),
                        src3, [ao_t], ["ao2"], q="gpd", stream=ao_t)
            units.append(tail)
            return units

        pending = []
        for g in range(4):
            dma(a16(QG, 8192), ap["q_dram"][g].rearrange("f p c -> p f c"),
                ["q_dram"], ["q_g"])
            for h in range(H):
                sset = h % 2
                dma(a16(KPB[sset], 2048), ap["kpb_dram"][h],
                    ["kpb_dram"], [f"kpb{sset}"])
                np_slots = 9
                nchunk = (len(pending) + np_slots - 1) // np_slots if pending else 0
                pi = [0]

                def emit_pending():
                    for th in pending[pi[0]:pi[0] + nchunk]:
                        th()
                    pi[0] += nchunk
                for j in range(4):
                    emit_pending()
                    softmax_A(g, h, j)
                    if j > 0:
                        emit_pending()
                        softmax_B(g, h, j - 1)
                emit_pending()
                softmax_B(g, h, 2 + 1)
                for th in pending[pi[0]:]:
                    th()
                if P3_SERIAL:
                    rec.barrier()
                pending = ptov_units(g, h)
                if P3_SERIAL:
                    for th in pending:
                        th()
                    pending = []
                    rec.barrier()
        for th in pending:
            th()
        rec.barrier()
        if "dbg_vtok" in ap:
            dma(ap["dbg_vtok2"][:], vtok[:], ["vtok"], ["dbg_vtok2"], q="gpd")
            rec.barrier()

        NB4 = 5
        AO4 = [0, 2048, 4096, 6144, 8192]
        XNT4 = [10240, 12288, 14336, 16384, 18432]
        WOCT = [20480, 28672]
        QFT = [36864, 45056]
        XSL = [0, 2048, 4096, 6144, 8192]

        def p4_wloads(cb, ct):
            for ch in range(NCH):
                dma(a16(WOCT[cb] + ch * 512, 512),
                    ap["WO"][ch * 128:(ch + 1) * 128, ct * 512:(ct + 1) * 512],
                    ["WO"], [f"wo_ct{cb}"], q="gpd")
            qft4 = a16(QFT[cb], 8192).rearrange("p (c ot o) -> p c ot o",
                                                 c=16, ot=4, o=128)
            for ot in range(4):
                dma(qft4[:, :, ot, :], ap["qfT_dram"][ct * 4 + ot],
                    ["qfT_dram"], [f"qft_ct{cb}"], q="scd")
        p4_wloads(0, 0)
        p4_wloads(1, 1)
        for pp in range(2):
            order = list(range(NT)) if pp == 0 else                 list(range(11, 16)) + list(range(11))
            for idx, tt in enumerate(order):
                tb = tt % NB4
                sb_ao = a16(AO4[tb], 2048)
                sb_x4 = a16(XNT4[tb], 2048)
                ao_t, x4_t = f"ao4{tb}", f"x4{tb}"
                if not (pp == 1 and idx < 5):
                    dma(sb_ao, ap["ao2"][tt], ["ao2"], [ao_t])
                    dma(sb_x4, ap["xnt2"][tt], ["xnt2"], [x4_t])
                xs = a32(XSL[tb], 1024)
                xs_t = f"xs{tb}"
                dma(xs, ap["X"][tt * 128:(tt + 1) * 128,
                                pp * 1024:(pp + 1) * 1024],
                    ["X"], [xs_t])
                ob = a32(XSL[tb] + 1024, 1024)
                ob_t = f"ob{tb}"
                for ctl in range(2):
                    ct = 2 * pp + ctl
                    cb = ctl
                    pq = (2 * idx + ctl) % 4
                    ps = ps_s[:, pq * 512:(pq + 1) * 512]
                    ps_t = f"pso{pq}"
                    for ch in range(NCH):
                        mm(ps, sb_ao[:, ch * 128:(ch + 1) * 128],
                           a16(WOCT[cb] + ch * 512, 512), ch == 0, False,
                           [ao_t, f"wo_ct{cb}"], [ps_t],
                           label=f"p4mm_ct{ct}_tt{tt}_ch{ch}")
                    for ich in range(NCH):
                        mm(ps, sb_x4[:, ich * 128:(ich + 1) * 128],
                           a16(QFT[cb] + ich * 512, 512), False, False,
                           [x4_t, f"qft_ct{cb}"], [ps_t])
                    mm(ps, sb_ones[:], sb_bias[0:1, ct * 512:(ct + 1) * 512],
                       False, True, ["sb_ones", "sb_bias"], [ps_t])
                    vop(lambda e, o=ob[:, ctl * 512:(ctl + 1) * 512], p=ps,
                        x=xs[:, ctl * 512:(ctl + 1) * 512]:
                            e.tensor_tensor(o, p, x, ALU.add),
                        [ps_t, xs_t], [ob_t])
                dma(ap["OUT"][tt * 128:(tt + 1) * 128,
                              pp * 1024:(pp + 1) * 1024],
                    ob, [ob_t], ["OUT"], q="gpd", stream=ob_t)
            if pp == 0:
                p4_wloads(0, 2)
                p4_wloads(1, 3)
        rec.barrier()
        rec.op("vector", lambda e: e.nop())

        dma_sems = {s: ec(nc.semaphore(f"d_{s}"))
                    for s in sorted(rec.stream_cnt)}
        rec.emit(nc, block, sems, dma_sems)
    return nc


def _prepare(L0_attn_x, L0_attn_Wq, L0_attn_Wk, L0_attn_Wv, L0_attn_Wo,
             L0_attn_head_scales, L0_ffn_weight, L0_ffn_bias):
    x = np.ascontiguousarray(np.asarray(L0_attn_x, dtype=np.float32))
    wq = np.ascontiguousarray(np.asarray(L0_attn_Wq, dtype=np.float32))
    wk = np.ascontiguousarray(np.asarray(L0_attn_Wk, dtype=np.float32))
    wv = np.ascontiguousarray(np.asarray(L0_attn_Wv, dtype=np.float32))
    wo = np.ascontiguousarray(np.asarray(L0_attn_Wo, dtype=np.float32))
    hs = np.asarray(L0_attn_head_scales, dtype=np.float32)
    wf = np.ascontiguousarray(np.asarray(L0_ffn_weight, dtype=np.float32))
    bias = np.ascontiguousarray(np.asarray(L0_ffn_bias, dtype=np.float32)
                                .reshape(1, C))
    hs_rep = np.ascontiguousarray(np.broadcast_to(hs[None, :], (128, H)))
    base = {
        "WQ": wq, "WK": wk, "WV": wv, "WO": wo, "WF": wf, "BIAS": bias,
        "HS": hs_rep,
        "cosT": _CONSTS["cosT"], "sinT": _CONSTS["sinT"],
        "ident": _CONSTS["ident"], "dirs": _CONSTS["dirs"],
        "ones1": _CONSTS["ones1"], "maskS": _CONSTS["maskS"],
        "maskG": _CONSTS["maskG"], "negb": _CONSTS["negb"],
    }
    in_maps = []
    for c in range(B):
        m = dict(base)
        m["X"] = np.ascontiguousarray(x[c])
        in_maps.append(m)
    return in_maps


def kernel(**inputs):
    in_maps = _prepare(**inputs)
    if "nc" not in _CACHED:
        _CACHED["nc"] = build_program()
    nc = _CACHED["nc"]
    res = run_bass_kernel_spmd(nc, in_maps, core_ids=list(range(B)))
    out = np.stack([np.asarray(r["OUT"], dtype=np.float32)
                    for r in res.results], axis=0)
    return out


# version 26 (final)
# speedup vs baseline: 1.0400x; 1.0359x over previous
import numpy as np
import ml_dtypes
import concourse.bass as bass
import concourse.mybir as mybir
from concourse.bass_utils import run_bass_kernel_spmd

F32 = mybir.dt.float32
BF16 = mybir.dt.bfloat16
AF = mybir.ActivationFunctionType
ALU = mybir.AluOpType
AX = mybir.AxisListType
bfd = ml_dtypes.bfloat16

B, T, C = 8, 2048, 2048
H, KVH, DH = 8, 2, 256
NT = T // 128
NCH = C // 128
NEG = -1.0e9
SCALE = 1.0 / 16.0

VENGS = ("sync", "tensor", "vector", "scalar", "gp", "gpd", "scd")
PHYS = {"sync": "sync", "tensor": "tensor", "vector": "vector",
        "scalar": "scalar", "gp": "gpsimd", "gpd": "gpsimd",
        "scd": "scalar"}


class _Op:
    __slots__ = ("veng", "fn", "dma", "stream", "deps", "dwaits",
                 "need_inc", "val", "idx", "label")

    def __init__(self, veng, fn, dma, stream):
        self.veng = veng
        self.fn = fn
        self.dma = dma
        self.stream = stream
        self.deps = set()
        self.dwaits = {}
        self.need_inc = False
        self.val = None
        self.idx = None
        self.label = None


class Rec:

    def __init__(self):
        self.phys_ops = {"sync": [], "tensor": [], "vector": [],
                         "scalar": [], "gpsimd": []}
        self.acc = {}
        self.fence_pending = {}
        self.stream_cnt = {}
        self.n_ops = 0

    def op(self, veng, fn, reads=(), writes=(), dma=False, stream=None,
           force_inc=False, label=None):
        assert dma == (stream is not None)
        o = _Op(veng, fn, dma, stream)
        o.label = label
        ph = PHYS[veng]
        deps = set()
        for t in reads:
            st = self.acc.get(t)
            if st:
                deps.update(st["w"].values())
        for t in writes:
            st = self.acc.get(t)
            if st:
                deps.update(st["w"].values())
                deps.update(st["r"])
        if ph in self.fence_pending:
            fence_ops, fence_streams = self.fence_pending.pop(ph)
            deps.update(fence_ops)
            for s, c in fence_streams.items():
                o.dwaits[s] = max(o.dwaits.get(s, 0), c)
        for p in deps:
            if p.dma:
                if o.stream is not None and p.stream == o.stream:
                    continue
                c = self.stream_cnt[p.stream]
                o.dwaits[p.stream] = max(o.dwaits.get(p.stream, 0), c)
            elif ph == "tensor" and PHYS[p.veng] == "tensor":
                pass
            else:
                o.deps.add(p)
                p.need_inc = True
        if dma:
            self.stream_cnt[stream] = self.stream_cnt.get(stream, 0) + 1
        if force_inc:
            o.need_inc = True
        for t in writes:
            st = self.acc.setdefault(t, {"w": {}, "r": []})
            st["w"] = {ph: o}
            st["r"] = []
        for t in reads:
            st = self.acc.setdefault(t, {"w": {}, "r": []})
            st["r"].append(o)
        self.phys_ops[ph].append(o)
        self.n_ops += 1
        return o

    def barrier(self):
        lasts = []
        for ph, ops in self.phys_ops.items():
            for o in reversed(ops):
                if not o.dma:
                    o.need_inc = True
                    lasts.append(o)
                    break
        snap = dict(self.stream_cnt)
        for ph in self.phys_ops:
            self.fence_pending[ph] = (set(lasts), snap)
        self.acc = {}

    def emit(self, nc, block, sems, dma_sems):
        cnt = {v: 0 for v in VENGS}
        for ph, ops in self.phys_ops.items():
            for o in ops:
                if o.need_inc:
                    cnt[o.veng] += 1
                    o.val = cnt[o.veng]

        def run_stream(eng, ph):
            guar = {}
            def wait(sem, val):
                k = id(sem)
                if guar.get(k, -1) < val:
                    eng.wait_ge(sem, val)
                    guar[k] = val
            for o in self.phys_ops[ph]:
                waits = {}
                for p in o.deps:
                    assert p.val is not None, "dep without inc"
                    waits[p.veng] = max(waits.get(p.veng, 0), p.val)
                for v, val in sorted(waits.items()):
                    wait(sems[v], val)
                for s, c in sorted(o.dwaits.items()):
                    wait(dma_sems[s], 16 * c)
                inst = o.fn(eng)
                if o.dma:
                    inst.then_inc(dma_sems[o.stream], 16)
                elif o.need_inc:
                    inst.then_inc(sems[o.veng], 1)

        @block.sync
        def _(sync):
            run_stream(sync, "sync")

        @block.tensor
        def _(tensor):
            run_stream(tensor, "tensor")

        @block.vector
        def _(vector):
            run_stream(vector, "vector")

        @block.scalar
        def _(scalar):
            run_stream(scalar, "scalar")

        @block.gpsimd
        def _(gpsimd):
            run_stream(gpsimd, "gpsimd")


def host_constants():
    d = {}
    inv = 1.0 / (10000.0 ** (np.arange(0, DH, 2, dtype=np.float64) / DH))
    t = np.arange(T, dtype=np.float64)
    f = t[:, None] * inv[None, :]
    d["cosT"] = np.cos(f).T.astype(bfd).copy()
    d["sinT"] = np.sin(f).T.astype(bfd).copy()
    d["ident"] = np.eye(128, dtype=bfd)
    trig = np.array([[sx, sy, sz] for sx in (1.0, -1.0) for sy in (1.0, -1.0)
                     for sz in (1.0, -1.0)], dtype=np.float64) / np.sqrt(3.0)
    dirs = np.zeros((128, 8), dtype=bfd)
    dirs[0:3, :] = trig.T.astype(bfd)
    d["dirs"] = dirs
    d["ones1"] = np.ones((1, 128), dtype=bfd)
    s = np.arange(512)[None, :]
    p = np.arange(128)[:, None]
    m = np.concatenate([np.where(s <= 128 * j + p, 0.0, NEG) for j in range(4)],
                       axis=1)
    d["maskS"] = m.astype(np.float32)
    d["maskG"] = m.astype(bfd)
    nb = np.empty((128, 2), dtype=np.float32)
    nb[:, 0] = -16.0
    nb[:, 1] = -20.0
    d["negb"] = nb
    return d


_CONSTS = host_constants()
_CACHED = {}


P3_SERIAL = False
P3_SLOTS = 11
P3_LAG = 6


def build_program(debug_outs=()):
    nc = bass.Bass("TRN2", target_bir_lowering=False, debug=False)
    ap = {}
    def inp(name, shape, dt):
        ap[name] = nc.dram_tensor(name, shape, dt, kind="ExternalInput").ap()
    inp("X", [T, C], F32)
    inp("WQ", [C, C], F32)
    inp("WK", [C, KVH * DH], F32)
    inp("WV", [C, KVH * DH], F32)
    inp("WO", [C, C], F32)
    inp("WF", [C, C], F32)
    inp("BIAS", [1, C], F32)
    inp("HS", [128, H], F32)
    inp("cosT", [128, T], BF16)
    inp("sinT", [128, T], BF16)
    inp("ident", [128, 128], BF16)
    inp("dirs", [128, H], BF16)
    inp("ones1", [1, 128], BF16)
    inp("maskS", [128, 2048], F32)
    inp("maskG", [128, 2048], BF16)
    inp("negb", [128, 2], F32)
    ap["OUT"] = nc.dram_tensor("OUT", [T, C], F32, kind="ExternalOutput").ap()
    def scratch(name, shape):
        kind = "ExternalOutput" if name in debug_outs else "Internal"
        ap[name] = nc.dram_tensor(name, shape, BF16, kind=kind).ap()
    scratch("xn_dram", [T, C])
    scratch("xnt2", [NT, 128, NCH, 128])
    scratch("qf_dram", [C, C])
    scratch("qfT_dram", [NCH, 128, NCH, 128])
    scratch("q_dram", [4, NCH, 128, 512])
    scratch("kpb_dram", [H, 128, T])
    scratch("v_dram", [KVH * DH, T])
    scratch("ao2", [NT, 128, NCH, 128])
    scratch("vT_dram", [T, KVH * DH])
    if "dbg_stats" in debug_outs:
        ap["dbg_stats"] = nc.dram_tensor("dbg_stats", [NT, 128, 16], F32,
                                         kind="ExternalOutput").ap()
    if "dbg_kt" in debug_outs:
        ap["dbg_kt"] = nc.dram_tensor("dbg_kt", [128, 4 * T], BF16,
                                      kind="ExternalOutput").ap()
        ap["dbg_qp"] = nc.dram_tensor("dbg_qp", [128, 128], BF16,
                                      kind="ExternalOutput").ap()
    if "dbg_vtok" in debug_outs:
        ap["dbg_vtok"] = nc.dram_tensor("dbg_vtok", [128, NT * 512], BF16,
                                        kind="ExternalOutput").ap()
        ap["dbg_vtok2"] = nc.dram_tensor("dbg_vtok2", [128, NT * 512], BF16,
                                         kind="ExternalOutput").ap()

    rec = Rec()

    from contextlib import ExitStack
    with ExitStack() as _ctx:
        ec = _ctx.enter_context
        sb_id = ec(nc.sbuf_tensor("sb_id", [128, 128], BF16))
        sb_ones = ec(nc.sbuf_tensor("sb_ones", [1, 128], BF16))
        sb_dirs = ec(nc.sbuf_tensor("sb_dirs", [128, H], BF16))
        sb_hs = ec(nc.sbuf_tensor("sb_hs", [128, H], F32))
        sb_hs2 = ec(nc.sbuf_tensor("sb_hs2", [128, 2 * H], F32))
        sb_negb = ec(nc.sbuf_tensor("sb_negb", [128, 2], F32))
        sb_sums = ec(nc.sbuf_tensor("sb_sums", [128, 16], F32))
        sb_qp = ec(nc.sbuf_tensor("sb_qp", [128, 128], F32))
        sb_bias = ec(nc.sbuf_tensor("sb_bias", [1, C], BF16))
        sb_maskS = ec(nc.sbuf_tensor("sb_maskS", [128, 2048], F32))
        sb_maskG = ec(nc.sbuf_tensor("sb_maskG", [128, 2048], BF16))
        sb_small = ec(nc.sbuf_tensor("sb_small", [128, 64], F32))
        sb_kprow = ec(nc.sbuf_tensor("sb_kprow", [1, T], BF16))
        sb_cos = ec(nc.sbuf_tensor("sb_cos", [128, T], BF16))
        sb_sin = ec(nc.sbuf_tensor("sb_sin", [128, T], BF16))
        kT = ec(nc.sbuf_tensor("kT", [128, 4 * T], BF16))
        vtok = ec(nc.sbuf_tensor("vtok", [128, NT * 512], BF16))
        AR16 = ec(nc.sbuf_tensor("AR16", [128, 54272], BF16))
        AR32 = ec(nc.sbuf_tensor("AR32", [128, 10240], F32))
        ps_s = ec(nc.psum_tensor("ps_s", [128, 2048], F32))
        ps_pt0 = ec(nc.psum_tensor("ps_pt0", [128, 512], F32))
        ps_pt1 = ec(nc.psum_tensor("ps_pt1", [128, 512], F32))
        ps_ov0 = ec(nc.psum_tensor("ps_ov0", [128, 512], F32))
        ps_ov1 = ec(nc.psum_tensor("ps_ov1", [128, 512], F32))
        s_sync = ec(nc.semaphore("s_sync"))
        s_tensor = ec(nc.semaphore("s_tensor"))
        s_vector = ec(nc.semaphore("s_vector"))
        s_scalar = ec(nc.semaphore("s_scalar"))
        s_gp = ec(nc.semaphore("s_gp"))
        s_gpd = ec(nc.semaphore("s_gpd"))
        s_scd = ec(nc.semaphore("s_scd"))
        block = ec(nc.Block())
        sems = {"sync": s_sync, "tensor": s_tensor, "vector": s_vector,
                "scalar": s_scalar, "gp": s_gp, "gpd": s_gpd, "scd": s_scd}

        def a16(lo, n):
            return AR16[:, lo:lo + n]

        def a32(lo, n):
            return AR32[:, lo:lo + n]

        small_slot = {}

        def slot(name):
            if name not in small_slot:
                small_slot[name] = len(small_slot)
            i = small_slot[name]
            assert i < 64
            return sb_small[:, i:i + 1]

        def dma(dst, src, reads, writes, q="sync", force_inc=False, stream=None):
            rec.op("gpd" if q == "gpd" else "sync",
                   lambda e, d=dst, s=src: e.dma_start(out=d, in_=s),
                   reads=reads, writes=writes, dma=True,
                   stream=stream or writes[0])

        def dmat(dst, src, reads, writes, stream=None, q="sync"):
            rec.op(q,
                   lambda e, d=dst, s=src: e.dma_start(out=d, in_=s, transpose=True),
                   reads=reads, writes=writes, dma=True,
                   stream=stream or writes[0])

        def mm(out, lhsT, rhs, start, stop, reads, writes, label=None):
            rec.op("tensor",
                   lambda e, o=out, l=lhsT, r=rhs, st=start, sp=stop:
                       e.matmul(o, l, r, start=st, stop=sp),
                   reads=reads, writes=writes, label=label)

        def vop(fn, reads, writes):
            rec.op("vector", fn, reads=reads, writes=writes)

        def sop(fn, reads, writes):
            rec.op("scalar", fn, reads=reads, writes=writes)

        dma(sb_id[:], ap["ident"][:], ["ident"], ["sb_id"], stream="init")
        dma(sb_dirs[:], ap["dirs"][:], ["dirs"], ["sb_dirs"], stream="init")
        dma(sb_ones[:], ap["ones1"][:], ["ones1"], ["sb_ones"], stream="init")
        dma(sb_hs[:], ap["HS"][:], ["HS"], ["sb_hs"], stream="init")
        dma(sb_maskS[:], ap["maskS"][:], ["maskS"], ["sb_maskS"], stream="init")
        dma(sb_maskG[:], ap["maskG"][:], ["maskG"], ["sb_maskG"], stream="init")
        dma(sb_cos[:], ap["cosT"][:], ["cosT"], ["sb_cos"], stream="init")
        dma(sb_sin[:], ap["sinT"][:], ["sinT"], ["sb_sin"], stream="init")
        dma(sb_bias[:], ap["BIAS"][:], ["BIAS"], ["sb_bias"], q="gpd")
        dma(sb_negb[:], ap["negb"][:], ["negb"], ["sb_negb"], stream="init")
        hs2v = sb_hs2[:].rearrange("p (a b) -> p a b", b=2)
        vop(lambda e: e.tensor_scalar(hs2v[:, :, 0], sb_hs[:], -1.0, 1.0,
                                      ALU.mult, ALU.add),
            ["sb_hs"], ["sb_hs2"])
        vop(lambda e: e.tensor_copy(hs2v[:, :, 1], sb_hs[:]),
            ["sb_hs"], ["sb_hs2"])

        WK, WV = 32768, 40960
        WQB = [16384, 24576]

        WQA = {0: WQB[0], 1: WQB[1], 2: WK, 3: WV}

        def cast_quarter(hp, emit=True):
            extra = [] if hp < 2 else (["wk"] if hp == 2 else ["wv"])
            ths = []
            for cch in range(NCH):
                def th(cch=cch, hp=hp, extra=extra):
                    dma(a16(WQA[hp] + cch * 512, 512),
                        ap["WQ"][cch * 128:(cch + 1) * 128,
                                 hp * 512:(hp + 1) * 512],
                        ["WQ"], extra + [f"wq{hp}"], q="gpd",
                        stream=f"wq{hp}")
                if emit:
                    th()
                else:
                    ths.append(th)
            return ths

        pending_casts = list(cast_quarter(0, emit=False))
        for cch in range(NCH):
            def thk(cch=cch):
                dma(a16(WK + cch * 512, 512),
                    ap["WK"][cch * 128:(cch + 1) * 128, :],
                    ["WK"], ["wk"], q="gpd")
            pending_casts.append(thk)
        for cch in range(NCH):
            def thv(cch=cch):
                dma(a16(WV + cch * 512, 512),
                    ap["WV"][cch * 128:(cch + 1) * 128, :],
                    ["WV"], ["wv"], q="gpd")
            pending_casts.append(thv)
        pending_casts += cast_quarter(1, emit=False)

        X_TILE = [0, 2048, 4096, 6144]
        WF_TILE = [8192]
        XN_BF = [0, 2048, 4096, 6144]
        GTP = [8192, 10240]
        LTN = [12288, 14336]

        def p1_stats(grp4):
            gb = grp4 % 2
            st = f"st{gb}"
            ssq4 = sb_small[:, gb * 32:gb * 32 + 4]
            sm4 = sb_small[:, gb * 32 + 4:gb * 32 + 8]
            mean4 = sb_small[:, gb * 32 + 8:gb * 32 + 12]
            msq4 = sb_small[:, gb * 32 + 12:gb * 32 + 16]
            varn4 = sb_small[:, gb * 32 + 16:gb * 32 + 20]
            rstd4 = sb_small[:, gb * 32 + 20:gb * 32 + 24]
            negmr4 = sb_small[:, gb * 32 + 24:gb * 32 + 28]
            vop(lambda e, o=mean4, x=sm4: e.tensor_scalar(o, x, 1.0 / C, None,
                                                          ALU.mult), [st], [st])
            vop(lambda e, o=msq4, x=mean4: e.tensor_tensor(o, x, x, ALU.mult),
                [st], [st])
            vop(lambda e, o=varn4, x=ssq4: e.tensor_scalar(o, x, 1.0 / C, None,
                                                           ALU.mult), [st], [st])
            vop(lambda e, o=varn4, x=msq4: e.tensor_tensor(o, o, x, ALU.subtract),
                [st], [st])
            vop(lambda e, o=varn4: e.tensor_scalar(o, o, 1e-5, None, ALU.add),
                [st], [st])
            sop(lambda e, o=rstd4, x=varn4: e.activation(o, x, AF.Sqrt),
                [st], [st])
            vop(lambda e, o=rstd4: e.reciprocal(o, o), [st], [st])
            vop(lambda e, o=negmr4, m=mean4, r=rstd4:
                    e.tensor_tensor(o, m, r, ALU.mult), [st], [st])
            vop(lambda e, o=negmr4: e.tensor_scalar(o, o, -1.0, None, ALU.mult),
                [st], [st])
            return rstd4, negmr4

        for i in range(NT):
            grp4, ii = i // 4, i % 4
            gb = grp4 % 2
            st = f"st{gb}"
            xb = i % 4
            xt = a32(X_TILE[xb], 2048)
            xt_t = f"xt{xb}"
            dma(xt, ap["X"][i * 128:(i + 1) * 128, :], ["X"], [xt_t])
            xnb = a16(XN_BF[xb], 2048)
            xnb_t = f"xnb{xb}"
            ssqc = sb_small[:, gb * 32 + ii:gb * 32 + ii + 1]
            smc = sb_small[:, gb * 32 + 4 + ii:gb * 32 + 4 + ii + 1]
            sop(lambda e, o=xnb, x=xt, a=ssqc: e.activation(o, x, AF.Square,
                                                            accum_out=a),
                [xt_t], [xnb_t, st])
            sop(lambda e, o=xnb, x=xt, a=smc: e.activation(o, x, AF.Identity,
                                                           accum_out=a),
                [xt_t], [xnb_t, st])
            if ii == 3:
                rstd4, negmr4 = p1_stats(grp4)
                for jj in range(4):
                    i2 = grp4 * 4 + jj
                    xb2 = i2 % 4
                    xt2 = a32(X_TILE[xb2], 2048)
                    xnb2 = a16(XN_BF[xb2], 2048)
                    xnb2_t = f"xnb{xb2}"
                    sop(lambda e, o=xnb2, x=xt2, r=rstd4[:, jj:jj + 1],
                        b=negmr4[:, jj:jj + 1]:
                            e.activation(o, x, AF.Identity, bias=b, scale=r),
                        [f"xt{xb2}", st], [xnb2_t])
                    xsg = vtok[:, (jj % 2) * 2048:(jj % 2) * 2048 + 2048]
                    xsg_t = f"xstg{jj % 2}"
                    for grp in range(4):
                        psb = [ps_pt0, ps_pt1][grp % 2]
                        pst = f"p1pt{grp % 2}"
                        for cc in range(4):
                            cch = grp * 4 + cc
                            mm(psb[:, cc * 128:(cc + 1) * 128],
                               xnb2[:, cch * 128:(cch + 1) * 128], sb_id[:],
                               True, True, [xnb2_t, "sb_id"], [pst])
                        ev = xsg[:, grp * 512:(grp + 1) * 512]
                        if grp % 2 == 0:
                            sop(lambda e, o=ev, x=psb[:]: e.copy(o, x),
                                [pst], [xsg_t])
                        else:
                            vop(lambda e, o=ev, x=psb[:]: e.tensor_copy(o, x),
                                [pst], [xsg_t])
                    dma(ap["xnt2"][i2], xsg, [xsg_t], ["xnt2"], q="gpd", stream=xsg_t)
            wt = a32(WF_TILE[0], 2048)
            wt_t = "wt0"
            dma(wt, ap["WF"][i * 128:(i + 1) * 128, :], ["WF"], [wt_t])
            wb = i % 2
            ths = sb_small[:, 60:61]
            thr = sb_small[:, 61:62]
            nthr = sb_small[:, 62:63]
            vop(lambda e, o=ths, x=wt: e.reduce_sum(o, x, AX.X,
                                                    apply_absolute_value=True),
                [wt_t], ["ths"])
            vop(lambda e, o=thr, x=ths: e.tensor_scalar(o, x, 0.7 / C, None, ALU.mult),
                ["ths"], ["thr"])
            vop(lambda e, o=nthr, x=thr: e.tensor_scalar(o, x, -1.0, None, ALU.mult),
                ["thr"], ["nthr"])
            gtp = a16(GTP[wb], 2048)
            ltn = a16(LTN[wb], 2048)
            vop(lambda e, o=gtp, x=wt, t=thr: e.tensor_scalar(o, x, t, None, ALU.is_gt),
                [wt_t, "thr"], [f"gtp{wb}"])
            vop(lambda e, o=ltn, x=wt, t=nthr: e.tensor_scalar(o, x, t, None, ALU.is_lt),
                [wt_t, "nthr"], [f"ltn{wb}"])
            vop(lambda e, o=gtp, x=ltn: e.tensor_tensor(o, o, x, ALU.subtract),
                [f"gtp{wb}", f"ltn{wb}"], [f"gtp{wb}"])
            qsg = vtok[:, 4096 + wb * 2048:4096 + wb * 2048 + 2048]
            qsg_t = f"qstg{wb}"
            for grp in range(4):
                psb = [ps_pt0, ps_pt1][grp % 2]
                pst = f"p1pt{grp % 2}"
                for cc in range(4):
                    ich = grp * 4 + cc
                    mm(psb[:, cc * 128:(cc + 1) * 128],
                       gtp[:, ich * 128:(ich + 1) * 128], sb_id[:],
                       True, True, [f"gtp{wb}", "sb_id"], [pst])
                ev = qsg[:, grp * 512:(grp + 1) * 512]
                if grp % 2 == 0:
                    sop(lambda e, o=ev, x=psb[:]: e.copy(o, x), [pst], [qsg_t])
                else:
                    vop(lambda e, o=ev, x=psb[:]: e.tensor_copy(o, x), [pst], [qsg_t])
            dma(ap["qfT_dram"][i], qsg, [qsg_t], ["qfT_dram"], q="gpd", stream=qsg_t)
            for th in pending_casts[i * 4:i * 4 + 4]:
                th()
        rec.barrier()

        XNT = [0, 8192]
        WRK = 49152
        RW = {}
        for par in range(2):
            base = WRK + par * 2560
            RW[par] = dict(
                qa_raw=a16(base + 0, 512), qb_raw=a16(base + 512, 512),
                qa2=a16(base + 1024, 512), qb2=a16(base + 1536, 512),
                tr1=a16(base + 2048, 512))

        def emit_kpb():
            kpb_sts = [vtok[:, 3072:5120], vtok[:, 5120:7168]]
            for h in range(H):
                kpb_st = kpb_sts[h % 2]
                kpb_st_t = f"kpb_st{h % 2}"
                kvc = 2 * (h // 4)
                for tt5 in range(4):
                    mm(ps_pt1[0:1, 0:512], sb_dirs[:, h:h + 1],
                       kT[:, kvc * T + tt5 * 512:kvc * T + (tt5 + 1) * 512],
                       True, True, ["sb_dirs", "kT"], ["pskp", "p2vt"])
                    vop(lambda e, o=sb_kprow[0:1, tt5 * 512:(tt5 + 1) * 512],
                        x=ps_pt1[0:1, 0:512]: e.tensor_copy(o, x),
                        ["pskp"], ["kprow"])
                for tt5 in range(4):
                    mm(ps_ov0[:], sb_ones[:], sb_kprow[0:1, tt5 * 512:(tt5 + 1) * 512],
                       True, True, ["sb_ones", "kprow"], ["pskpb"])
                    sop(lambda e, o=kpb_st[:, tt5 * 512:(tt5 + 1) * 512], x=ps_ov0[:]:
                            e.copy(o, x), ["pskpb"], [kpb_st_t])
                dma(ap["kpb_dram"][h], kpb_st, [kpb_st_t], ["kpb_dram"], q="gpd", stream=kpb_st_t)

        qp_pending = []
        for hpp in range(2):
            if hpp == 1:
                emit_kpb()
                cast_quarter(2)
                cast_quarter(3)
            for tt5 in range(4):
                xb = (hpp * 4 + tt5) % 2
                xnt = a16(XNT[xb], 8192)
                xnt_t = f"xnt{xb}"
                xnt4 = xnt.rearrange("p (c gg t) -> p c gg t", c=16, gg=4)
                for gg in range(4):
                    dma(xnt4[:, :, gg, :], ap["xnt2"][tt5 * 4 + gg],
                        ["xnt2"], [xnt_t], q="sync" if gg % 2 == 0 else "scd")
                cs = sb_cos[:, tt5 * 512:(tt5 + 1) * 512]
                sn = sb_sin[:, tt5 * 512:(tt5 + 1) * 512]

                def rope_pair(par, psA, psB, tA, tB, outA, outA_t, outB, outB_t,
                              cs=cs, sn=sn):
                    w = RW[par]
                    qa_raw, qb_raw = w["qa_raw"], w["qb_raw"]
                    qa2, qb2, tr1 = w["qa2"], w["qb2"], w["tr1"]
                    ta, tb = f"qa_raw{par}", f"qb_raw{par}"
                    t2a, t2b, ttr = f"qa2_{par}", f"qb2_{par}", f"tr1_{par}"
                    sop(lambda e, o=qa_raw, x=psA: e.copy(o, x), [tA], [ta])
                    sop(lambda e, o=qb_raw, x=psB: e.copy(o, x), [tB], [tb])
                    vop(lambda e, o=qa2, x=qa_raw, c=cs: e.tensor_tensor(o, x, c, ALU.mult),
                        [ta, "sb_cos"], [t2a])
                    vop(lambda e, o=tr1, x=qb_raw, c=sn: e.tensor_tensor(o, x, c, ALU.mult),
                        [tb, "sb_sin"], [ttr])
                    vop(lambda e, o=outA, x=qa2, y=tr1: e.tensor_tensor(o, x, y, ALU.subtract),
                        [t2a, ttr], [outA_t])
                    vop(lambda e, o=qb2, x=qb_raw, c=cs: e.tensor_tensor(o, x, c, ALU.mult),
                        [tb, "sb_cos"], [t2b])
                    vop(lambda e, o=tr1, x=qa_raw, c=sn: e.tensor_tensor(o, x, c, ALU.mult),
                        [ta, "sb_sin"], [ttr])
                    vop(lambda e, o=outB, x=qb2, y=tr1: e.tensor_tensor(o, x, y, ALU.add),
                        [t2b, ttr], [outB_t])

                for hh4 in range(4):
                    hp = 2 * hpp + hh4 // 2
                    hh = hh4 % 2
                    h = 2 * hp + hh
                    par = hh
                    psA = ps_s[:, par * 1024:par * 1024 + 512]
                    psB = ps_s[:, par * 1024 + 512:par * 1024 + 1024]
                    tA, tB = f"psq{par}a", f"psq{par}b"
                    for cch in range(NCH):
                        lA = a16(WQA[hp] + cch * 512 + hh * 256, 128)
                        lB = a16(WQA[hp] + cch * 512 + hh * 256 + 128, 128)
                        r = a16(XNT[xb] + cch * 512, 512)
                        mm(psA, lA, r, cch == 0, cch == NCH - 1,
                           [f"wq{hp}", xnt_t], [tA])
                        mm(psB, lB, r, cch == 0, cch == NCH - 1,
                           [f"wq{hp}", xnt_t], [tB])
                    for th in qp_pending:
                        th()
                    qp_pending = []
                    qa2, qb2 = RW[par]["qa2"], RW[par]["qb2"]
                    t2a, t2b = f"qa2_{par}", f"qb2_{par}"
                    rope_pair(par, psA, psB, tA, tB, qa2, t2a, qb2, t2b)
                    dma(ap["q_dram"][tt5, 2 * h], qa2, [t2a], ["q_dram"], q="gpd", stream=t2a)
                    dma(ap["q_dram"][tt5, 2 * h + 1], qb2, [t2b], ["q_dram"], q="gpd", stream=t2b)

                    def qp_thunk(h=h, hh=hh, tt5=tt5, qa2=qa2, t2a=t2a):
                        for j in range(4):
                            mm(ps_pt0[:, hh * 4 + j:hh * 4 + j + 1],
                               qa2[:, j * 128:(j + 1) * 128],
                               sb_dirs[:, h:h + 1], True, True,
                               [t2a, "sb_dirs"], ["psqp"])
                        vop(lambda e,
                            o=sb_qp[:, h * 16 + tt5 * 4:h * 16 + tt5 * 4 + 4],
                            x=ps_pt0[:, hh * 4:hh * 4 + 4]: e.tensor_copy(o, x),
                            ["psqp"], ["sb_qp"])
                    qp_pending = [qp_thunk]
                for th in qp_pending:
                    th()
                qp_pending = []
                if hpp == 0:
                    for kvp in range(KVH):
                        par = kvp % 2
                        psA = ps_s[:, par * 1024:par * 1024 + 512]
                        psB = ps_s[:, par * 1024 + 512:par * 1024 + 1024]
                        tA, tB = f"psq{par}a", f"psq{par}b"
                        for cch in range(NCH):
                            lA = a16(WK + cch * 512 + kvp * 256, 128)
                            lB = a16(WK + cch * 512 + kvp * 256 + 128, 128)
                            r = a16(XNT[xb] + cch * 512, 512)
                            mm(psA, lA, r, cch == 0, cch == NCH - 1,
                               ["wk", xnt_t], [tA])
                            mm(psB, lB, r, cch == 0, cch == NCH - 1,
                               ["wk", xnt_t], [tB])
                        oA = kT[:, (2 * kvp) * T + tt5 * 512:(2 * kvp) * T + (tt5 + 1) * 512]
                        oB = kT[:, (2 * kvp + 1) * T + tt5 * 512:(2 * kvp + 1) * T + (tt5 + 1) * 512]
                        rope_pair(par, psA, psB, tA, tB, oA, "kT", oB, "kT")
                    for vf in range(4):
                        for cch in range(NCH):
                            l = a16(WV + cch * 512 + vf * 128, 128)
                            r = a16(XNT[xb] + cch * 512, 512)
                            mm(ps_ov1[:], l, r, cch == 0, cch == NCH - 1,
                               ["wv", xnt_t], ["psv"])
                        vs = vtok[:, vf * 512:(vf + 1) * 512]
                        vs_t = f"vs4_{vf}"
                        sop(lambda e, o=vs, x=ps_ov1[:]: e.copy(o, x),
                            ["psv"], [vs_t])
                    for tb4 in range(4):
                        for vf in range(4):
                            mm(ps_pt1[:, vf * 128:(vf + 1) * 128],
                               vtok[:, vf * 512 + tb4 * 128:
                                     vf * 512 + (tb4 + 1) * 128],
                               sb_id[:], True, True,
                               [f"vs4_{vf}", "sb_id"], ["p2vt"])
                        vts = vtok[:, 2048 + (tb4 % 2) * 512:
                                   2048 + (tb4 % 2) * 512 + 512]
                        vts_t = f"vts{tb4 % 2}"
                        if tb4 % 2 == 0:
                            sop(lambda e, o=vts: e.copy(o, ps_pt1[:]),
                                ["p2vt"], [vts_t])
                        else:
                            vop(lambda e, o=vts: e.tensor_copy(o, ps_pt1[:]),
                                ["p2vt"], [vts_t])
                        dma(ap["vT_dram"][tt5 * 512 + tb4 * 128:
                                          tt5 * 512 + (tb4 + 1) * 128, :],
                            vts, [vts_t], ["vT_dram"], q="gpd", stream=vts_t)
        for th in qp_pending:
            th()
        qp_pending = []
        rec.barrier()

        if "dbg_kt" in ap:
            dma(ap["dbg_kt"][:], kT[:], ["kT"], ["dbg_kt"], q="gpd")
            dma(ap["dbg_qp"][:], sb_qp[:], ["sb_qp"], ["dbg_qp"], q="gpd")
            rec.barrier()
        PC = 0
        PTG = 16384
        QG = 24576
        KPB = [32768, 34816]
        EXPS = [36864, 38912]
        RAWG = [40960, 43008]
        EXPG = [45056, 47104]
        AOST = [49152, 50176]

        for tc in range(NT):
            dma(vtok[:, tc * 512:(tc + 1) * 512],
                ap["vT_dram"][tc * 128:(tc + 1) * 128, :],
                ["vT_dram"], ["vtok"])

        if "dbg_vtok" in ap:
            rec.barrier()
            dma(ap["dbg_vtok"][:], vtok[:], ["vtok"], ["dbg_vtok"], q="gpd")
            rec.barrier()

        def softmax_A(g, h, j):
            qt = 4 * g + j
            sset = h % 2
            jb = j % 2
            ndw = g * 512
            klen = (g + 1) * 512
            qg_lA = a16(QG + (2 * h) * 512 + j * 128, 128)
            qg_lB = a16(QG + (2 * h + 1) * 512 + j * 128, 128)
            kvc = 2 * (h // 4)
            exps = a16(EXPS[jb], 2048)
            exps_t = f"expS{jb}"
            sums = sb_sums[:, 8 * jb:8 * jb + 8]
            sums_t = f"sums{jb}"
            sbase = jb * 2 if g <= 1 else 0
            for kt in range(g + 1):
                rhsA = kT[:, kvc * T + kt * 512:kvc * T + (kt + 1) * 512]
                rhsB = kT[:, (kvc + 1) * T + kt * 512:(kvc + 1) * T + (kt + 1) * 512]
                o = ps_s[:, (sbase + kt) * 512:(sbase + kt + 1) * 512]
                bt = "ps_s%d" % (sbase + kt)
                mm(o, qg_lA, rhsA, True, False, ["q_g", "kT"], [bt])
                mm(o, qg_lB, rhsB, False, True, ["q_g", "kT"], [bt])
                if kt == g:
                    vop(lambda e, o=o: e.tensor_tensor(
                            o, o, sb_maskS[:, j * 512:(j + 1) * 512], ALU.add),
                        [bt, "sb_maskS"], [bt])
                sop(lambda e, o=exps[:, kt * 512:(kt + 1) * 512], x=o,
                    a=sums[:, kt:kt + 1]:
                        e.activation(o, x, AF.Exp, bias=sb_negb[:, 0:1],
                                     scale=SCALE, accum_out=a),
                    [bt, "sb_negb"], [exps_t, sums_t])
            rg = a16(RAWG[jb], 2048)
            rg_t = f"rawG{jb}"
            qp_col = sb_qp[:, h * 16 + qt:h * 16 + qt + 1]
            kpb = a16(KPB[sset], 2048)
            vop(lambda e, o=rg[:, 0:klen], x=kpb[:, 0:klen], s=qp_col:
                    e.tensor_scalar(o, x, s, None, ALU.mult),
                [f"kpb{sset}", "sb_qp"], [rg_t])
            vop(lambda e, o=rg[:, ndw:klen]:
                    e.tensor_tensor(o, o, sb_maskG[:, j * 512:(j + 1) * 512],
                                    ALU.add),
                [rg_t, "sb_maskG"], [rg_t])
            expg = a16(EXPG[jb], 2048)
            expg_t = f"expG{jb}"
            sop(lambda e, o=expg[:, 0:klen], x=rg[:, 0:klen], a=sums[:, 5:6]:
                    e.activation(o, x, AF.Exp, bias=sb_negb[:, 1:2], accum_out=a),
                [rg_t, "sb_negb"], [expg_t, sums_t])
        def softmax_B(g, h, j):
            qt = 4 * g + j
            sset = h % 2
            jb = j % 2
            klen = (g + 1) * 512
            exps = a16(EXPS[jb], 2048)
            exps_t = f"expS{jb}"
            expg = a16(EXPG[jb], 2048)
            expg_t = f"expG{jb}"
            sums = sb_sums[:, 8 * jb:8 * jb + 8]
            sums_t = f"sums{jb}"
            if g == 0:
                vop(lambda e, o=sums[:, 4:5], x=sums[:, 0:1]: e.tensor_copy(o, x),
                    [sums_t], [sums_t])
            else:
                vop(lambda e, o=sums[:, 4:5], x=sums[:, 0:g + 1], gg=g:
                        e.reduce_sum(o, x, AX.X),
                    [sums_t], [sums_t])
            vop(lambda e, o=sums[:, 6:8], x=sums[:, 4:6]: e.reciprocal(o, x),
                [sums_t], [sums_t])
            vop(lambda e, o=sums[:, 6:8], x=sb_hs2[:, 2 * h:2 * h + 2]:
                    e.tensor_tensor(o, o, x, ALU.mult),
                [sums_t, "sb_hs2"], [sums_t])
            vop(lambda e, o=expg[:, 0:klen], s=sums[:, 7:8]:
                    e.tensor_scalar(o, o, s, None, ALU.mult),
                [expg_t, sums_t], [expg_t])
            pc = a16(PC + (sset * 4 + j) * 2048, 2048)
            vop(lambda e, o=exps[:, 0:klen], s=sums[:, 6:7]:
                    e.tensor_scalar(o, o, s, None, ALU.mult),
                [exps_t, sums_t], [exps_t])
            vop(lambda e, o=pc[:, 0:klen], x=exps[:, 0:klen], y=expg[:, 0:klen]:
                    e.tensor_tensor(o, x, y, ALU.add),
                [exps_t, expg_t], [f"pc{sset}_{j}"])

        def ptov_units(g, h):
            sset = h % 2
            kvc4 = (h // 4) * 256
            nkc = 4 * g + 4
            units = []

            def memsets():
                for j0 in range(1, 4):
                    kc = 4 * g + j0
                    o = a16(PTG + kc * 512, j0 * 128)
                    vop(lambda e, o=o: e.memset(o, 0.0), [], ["ptg"])
            units.append(memsets)

            def tp_unit(kc):
                j0 = max(0, kc - 4 * g)
                pst = [ps_pt0, ps_pt1][kc % 2]
                pst_t = f"ps_pt{kc % 2}"
                for j in range(j0, 4):
                    pc = a16(PC + (sset * 4 + j) * 2048, 2048)
                    mm(pst[:, j * 128:(j + 1) * 128],
                       pc[:, kc * 128:(kc + 1) * 128], sb_id[:],
                       True, True, [f"pc{sset}_{j}", "sb_id"], [pst_t])
                ev = a16(PTG + kc * 512 + j0 * 128, 512 - j0 * 128)
                if kc % 2 == 0:
                    sop(lambda e, o=ev, x=pst[:, j0 * 128:512]: e.copy(o, x),
                        [pst_t], ["ptg"])
                else:
                    vop(lambda e, o=ev, x=pst[:, j0 * 128:512]: e.tensor_copy(o, x),
                        [pst_t], ["ptg"])

            def ov_unit(kc):
                lhs0 = vtok[:, kc * 512 + kvc4:kc * 512 + kvc4 + 128]
                lhs1 = vtok[:, kc * 512 + kvc4 + 128:kc * 512 + kvc4 + 256]
                rhs = a16(PTG + kc * 512, 512)
                mm(ps_ov0[:], lhs0, rhs, kc == 0, kc == nkc - 1,
                   ["vtok", "ptg"], ["ps_ov0"])
                mm(ps_ov1[:], lhs1, rhs, kc == 0, kc == nkc - 1,
                   ["vtok", "ptg"], ["ps_ov1"])

            LAG = P3_LAG
            seq = []
            for kc in range(nkc):
                seq.append(("tp", kc))
                if kc >= LAG:
                    seq.append(("ov", kc - LAG))
            for kc in range(max(0, nkc - LAG), nkc):
                seq.append(("ov", kc))
            for kind, kc in seq:
                def unit(kind=kind, kc=kc):
                    (tp_unit if kind == "tp" else ov_unit)(kc)
                units.append(unit)

            def tail():
                ab = h % 2
                ao = a16(AOST[ab], 1024)
                ao_t = f"ao{ab}"
                sop(lambda e, o=ao[:, 0:512], x=ps_ov0[:]: e.copy(o, x),
                    ["ps_ov0"], [ao_t])
                sop(lambda e, o=ao[:, 512:1024], x=ps_ov1[:]: e.copy(o, x),
                    ["ps_ov1"], [ao_t])
                for dch in range(2):
                    src3 = ao[:, dch * 512:(dch + 1) * 512].rearrange(
                        "p (g2 t) -> p g2 t", g2=4)
                    dma(ap["ao2"][4 * g:4 * g + 4, :, 2 * h + dch]
                        .rearrange("g2 p t -> p g2 t"),
                        src3, [ao_t], ["ao2"], q="gpd", stream=ao_t)
            units.append(tail)
            return units

        pending = []
        for g in range(4):
            dma(a16(QG, 8192), ap["q_dram"][g].rearrange("f p c -> p f c"),
                ["q_dram"], ["q_g"])
            for h in range(H):
                sset = h % 2
                dma(a16(KPB[sset], 2048), ap["kpb_dram"][h],
                    ["kpb_dram"], [f"kpb{sset}"])
                np_slots = P3_SLOTS
                nchunk = (len(pending) + np_slots - 1) // np_slots if pending else 0
                pi = [0]

                def emit_pending():
                    for th in pending[pi[0]:pi[0] + nchunk]:
                        th()
                    pi[0] += nchunk
                for j in range(4):
                    emit_pending()
                    softmax_A(g, h, j)
                    if j > 0:
                        emit_pending()
                        softmax_B(g, h, j - 1)
                emit_pending()
                softmax_B(g, h, 2 + 1)
                for th in pending[pi[0]:]:
                    th()
                if P3_SERIAL:
                    rec.barrier()
                pending = ptov_units(g, h)
                if P3_SERIAL:
                    for th in pending:
                        th()
                    pending = []
                    rec.barrier()
        for th in pending:
            th()
        rec.barrier()
        if "dbg_vtok" in ap:
            dma(ap["dbg_vtok2"][:], vtok[:], ["vtok"], ["dbg_vtok2"], q="gpd")
            rec.barrier()

        NB4 = 5
        AO4 = [0, 2048, 4096, 6144, 8192]
        XNT4 = [10240, 12288, 14336, 16384, 18432]
        WOCT = [20480, 28672]
        QFT = [36864, 45056]
        XSL = [0, 2048, 4096, 6144, 8192]

        def p4_wloads(cb, ct):
            for ch in range(NCH):
                dma(a16(WOCT[cb] + ch * 512, 512),
                    ap["WO"][ch * 128:(ch + 1) * 128, ct * 512:(ct + 1) * 512],
                    ["WO"], [f"wo_ct{cb}"], q="gpd")
            qft4 = a16(QFT[cb], 8192).rearrange("p (c ot o) -> p c ot o",
                                                 c=16, ot=4, o=128)
            for ot in range(4):
                dma(qft4[:, :, ot, :], ap["qfT_dram"][ct * 4 + ot],
                    ["qfT_dram"], [f"qft_ct{cb}"], q="scd")
        p4_wloads(0, 0)
        p4_wloads(1, 1)
        for pp in range(2):
            order = list(range(NT)) if pp == 0 else                 list(range(11, 16)) + list(range(11))
            for idx, tt in enumerate(order):
                tb = tt % NB4
                sb_ao = a16(AO4[tb], 2048)
                sb_x4 = a16(XNT4[tb], 2048)
                ao_t, x4_t = f"ao4{tb}", f"x4{tb}"
                if not (pp == 1 and idx < 5):
                    dma(sb_ao, ap["ao2"][tt], ["ao2"], [ao_t])
                    dma(sb_x4, ap["xnt2"][tt], ["xnt2"], [x4_t])
                xs = a32(XSL[tb], 1024)
                xs_t = f"xs{tb}"
                dma(xs, ap["X"][tt * 128:(tt + 1) * 128,
                                pp * 1024:(pp + 1) * 1024],
                    ["X"], [xs_t])
                ob = a32(XSL[tb] + 1024, 1024)
                ob_t = f"ob{tb}"
                for ctl in range(2):
                    ct = 2 * pp + ctl
                    cb = ctl
                    pq = (2 * idx + ctl) % 4
                    ps = ps_s[:, pq * 512:(pq + 1) * 512]
                    ps_t = f"pso{pq}"
                    for ch in range(NCH):
                        mm(ps, sb_ao[:, ch * 128:(ch + 1) * 128],
                           a16(WOCT[cb] + ch * 512, 512), ch == 0, False,
                           [ao_t, f"wo_ct{cb}"], [ps_t],
                           label=f"p4mm_ct{ct}_tt{tt}_ch{ch}")
                    for ich in range(NCH):
                        mm(ps, sb_x4[:, ich * 128:(ich + 1) * 128],
                           a16(QFT[cb] + ich * 512, 512), False, False,
                           [x4_t, f"qft_ct{cb}"], [ps_t])
                    mm(ps, sb_ones[:], sb_bias[0:1, ct * 512:(ct + 1) * 512],
                       False, True, ["sb_ones", "sb_bias"], [ps_t])
                    vop(lambda e, o=ob[:, ctl * 512:(ctl + 1) * 512], p=ps,
                        x=xs[:, ctl * 512:(ctl + 1) * 512]:
                            e.tensor_tensor(o, p, x, ALU.add),
                        [ps_t, xs_t], [ob_t])
                dma(ap["OUT"][tt * 128:(tt + 1) * 128,
                              pp * 1024:(pp + 1) * 1024],
                    ob, [ob_t], ["OUT"], q="gpd", stream=ob_t)
            if pp == 0:
                p4_wloads(0, 2)
                p4_wloads(1, 3)
        rec.barrier()
        rec.op("vector", lambda e: e.nop())

        dma_sems = {s: ec(nc.semaphore(f"d_{s}"))
                    for s in sorted(rec.stream_cnt)}
        rec.emit(nc, block, sems, dma_sems)
    return nc


def _prepare(L0_attn_x, L0_attn_Wq, L0_attn_Wk, L0_attn_Wv, L0_attn_Wo,
             L0_attn_head_scales, L0_ffn_weight, L0_ffn_bias):
    x = np.ascontiguousarray(np.asarray(L0_attn_x, dtype=np.float32))
    wq = np.ascontiguousarray(np.asarray(L0_attn_Wq, dtype=np.float32))
    wk = np.ascontiguousarray(np.asarray(L0_attn_Wk, dtype=np.float32))
    wv = np.ascontiguousarray(np.asarray(L0_attn_Wv, dtype=np.float32))
    wo = np.ascontiguousarray(np.asarray(L0_attn_Wo, dtype=np.float32))
    hs = np.asarray(L0_attn_head_scales, dtype=np.float32)
    wf = np.ascontiguousarray(np.asarray(L0_ffn_weight, dtype=np.float32))
    bias = np.ascontiguousarray(np.asarray(L0_ffn_bias, dtype=np.float32)
                                .reshape(1, C))
    hs_rep = np.ascontiguousarray(np.broadcast_to(hs[None, :], (128, H)))
    base = {
        "WQ": wq, "WK": wk, "WV": wv, "WO": wo, "WF": wf, "BIAS": bias,
        "HS": hs_rep,
        "cosT": _CONSTS["cosT"], "sinT": _CONSTS["sinT"],
        "ident": _CONSTS["ident"], "dirs": _CONSTS["dirs"],
        "ones1": _CONSTS["ones1"], "maskS": _CONSTS["maskS"],
        "maskG": _CONSTS["maskG"], "negb": _CONSTS["negb"],
    }
    in_maps = []
    for c in range(B):
        m = dict(base)
        m["X"] = np.ascontiguousarray(x[c])
        in_maps.append(m)
    return in_maps


def kernel(**inputs):
    in_maps = _prepare(**inputs)
    if "nc" not in _CACHED:
        _CACHED["nc"] = build_program()
    nc = _CACHED["nc"]
    res = run_bass_kernel_spmd(nc, in_maps, core_ids=list(range(B)))
    out = np.stack([np.asarray(r["OUT"], dtype=np.float32)
                    for r in res.results], axis=0)
    return out
